# version 08
# baseline (speedup 1.0000x reference)
import math
import numpy as np
import ml_dtypes
from contextlib import ExitStack

import concourse.bass as bass
import concourse.mybir as mybir
from concourse.bass_utils import run_bass_kernel_spmd

F32 = mybir.dt.float32
BF16 = mybir.dt.bfloat16
I32 = mybir.dt.int32
AF = mybir.ActivationFunctionType
ALU = mybir.AluOpType
AX = mybir.AxisListType

T = 2048
D = 1024
HID = 4096
NTC = T // 128
NDC = D // 128
NHC = HID // 128
GA = 256
NGA = T // GA
GB = 512
NGB = T // GB
NHG = 4
NKC = T // 128
QHW = 1024 if T >= 1024 else T
NQH = T // QHW

SBUF_BASE = 18432
SBUF_CAP = 229376

_TWO_PI = 2.0 * math.pi
_C1 = math.floor(_TWO_PI * 512.0) / 512.0
_C2 = float(np.float32(math.floor((_TWO_PI - _C1) * (1 << 22)) / (1 << 22)))
_C3 = _TWO_PI - _C1 - _C2
_PI_F32 = float(np.float32(math.pi))
_PIH_HI = float(np.float32(math.pi / 2))
_PIH_LO = math.pi / 2 - _PIH_HI
_LN_HI = float(np.float32(math.log(10000.0)))
_LN_LO = math.log(10000.0) - _LN_HI

CN = {"half": 0, "neghalf": 1, "neg15": 2, "one": 3}


def _set_T(newT):
    global T, NTC, NGA, NGB, NKC, QHW, NQH
    T = newT
    NTC = T // 128
    NGA = T // GA
    NGB = T // GB
    NKC = T // 128
    QHW = 1024 if T >= 1024 else T
    NQH = T // QHW


class Sched:
    ENGINES = ("tensor", "vector", "scalar", "gpsimd", "sync")
    DKEY = {"sync": "syncd", "gpsimd": "gpsd", "vector": "vecd",
            "scalar": "scad", "tensor": "tend"}
    KEYS = ENGINES + ("syncd", "gpsd", "vecd", "scad", "tend")

    def __init__(self):
        self.steps = {e: [] for e in self.ENGINES}
        self.cnt = {k: 0 for k in self.KEYS}

    def add(self, eng, fn, waits=(), inc=True, dma=False):
        w = []
        for it in waits:
            if not it:
                continue
            k, v = it
            if k != eng and v > 0:
                w.append((k, v))
        self.steps[eng].append((fn, w, inc, dma))
        key = self.DKEY[eng] if dma else eng
        if inc:
            self.cnt[key] += 16 if dma else 1
        return (key, self.cnt[key])

    def evt(self, eng):
        return (eng, self.cnt[eng])

    def snap(self):
        return [(k, self.cnt[k]) for k in self.KEYS]

    def barrier(self):
        s = self.snap()
        for e in self.ENGINES:
            self.add(e, "SEMINC", waits=s)
        return self.snap()

    def emit(self, block, sems):
        for eng in self.ENGINES:
            steps = self.steps[eng]
            if not steps:
                continue

            def body(h, steps=steps, eng=eng):
                hi = {k: 0 for k in self.KEYS}
                dkey = self.DKEY[eng]
                done = 0
                for fn, waits, inc, dma in steps:
                    if eng == "vector" and done > hi[eng]:
                        h.wait_ge(sems[eng], done)
                        hi[eng] = done
                    for (wk, wv) in waits:
                        if wv > hi[wk]:
                            h.wait_ge(sems[wk], wv)
                            hi[wk] = wv
                    if fn == "SEMINC":
                        h.sem_inc(sems[eng], 1)
                        done += 1
                        continue
                    ins = fn(h)
                    if inc:
                        ins.then_inc(sems[dkey if dma else eng], 16 if dma else 1)
                        if not dma:
                            done += 1

            getattr(block, eng)(body)


class Arena:
    def __init__(self, nc):
        self.nc = nc
        self.off = SBUF_BASE
        self.n = 0
        self.peak = 0

    def alloc(self, shape, dtype, align=64, at=None):
        nbytes = int(np.prod(shape[1:])) * mybir.dt.size(dtype)
        if at is None:
            self.off = (self.off + align - 1) // align * align
            off = self.off
            self.off += nbytes
            self.peak = max(self.peak, self.off)
            assert self.off <= SBUF_CAP, f"SBUF overflow: {self.off}"
        else:
            off = at
        t = self.nc.alloc_sbuf_tensor_at(
            f"sb{self.n}", list(shape), dtype, offset=off)
        self.n += 1
        self.last = off
        return t

    def mark(self):
        return self.off

    def reset(self, mark):
        self.off = mark


def build_program():
    nc = bass.Bass("TRN2", target_bir_lowering=False, debug=False,
                   detect_race_conditions=False)

    x_in = nc.dram_tensor("x_in", [2, T, D], F32, kind="ExternalInput").ap()
    w_qkv = nc.dram_tensor("w_qkv", [D, 3 * D], F32, kind="ExternalInput").ap()
    w_o = nc.dram_tensor("w_o", [D, D], F32, kind="ExternalInput").ap()
    w1t_d = nc.dram_tensor("w1t", [D, HID], F32, kind="ExternalInput").ap()
    w2t_d = nc.dram_tensor("w2t", [HID, D], F32, kind="ExternalInput").ap()
    cwt = nc.dram_tensor("cwt", [128, 3 * NHC], F32, kind="ExternalInput").ap()
    cbt = nc.dram_tensor("cbt", [128, NHC], F32, kind="ExternalInput").ap()
    identin = nc.dram_tensor("identin", [128, 128], BF16, kind="ExternalInput").ap()
    ident16in = nc.dram_tensor("ident16in", [16, 16], F32, kind="ExternalInput").ap()
    constin = nc.dram_tensor("constin", [128, 8], F32, kind="ExternalInput").ap()
    x_out = nc.dram_tensor("x_out", [2, T, D], F32, kind="ExternalOutput").ap()

    wq1d = nc.dram_tensor("wq1d", [D, HID], BF16, kind="Internal").ap()
    wq2d = nc.dram_tensor("wq2d", [HID, D], BF16, kind="Internal").ap()
    wqkvd = nc.dram_tensor("wqkvd", [D, 3 * D], BF16, kind="Internal").ap()
    wod = nc.dram_tensor("wod", [D, D], BF16, kind="Internal").ap()
    hqd = nc.dram_tensor("hqd", [NHC, 128, T], BF16, kind="Internal").ap()
    statd = nc.dram_tensor("statd", [2 * T], F32, kind="Internal").ap()
    roped = nc.dram_tensor("roped", [2, 32, T], F32, kind="Internal").ap()

    sc = Sched()
    ar = Arena(nc)
    psum_stk = []

    def psopen():
        psum_stk.append(ExitStack())
        return psum_stk[-1]

    def psclose():
        psum_stk.pop().close()

    uid = [0]

    def psum(stk, shape, dtype=F32):
        uid[0] += 1
        return stk.enter_context(nc.psum_tensor(f"ps{uid[0]}", list(shape), dtype))

    xt = ar.alloc([128, NTC * D], F32)
    consts = ar.alloc([128, 8], F32)
    ident = ar.alloc([128, 128], BF16)
    ident16 = ar.alloc([16, 16], F32)
    cw = ar.alloc([128, 3 * NHC], F32)
    cb = ar.alloc([128, NHC], F32)
    smallv = ar.alloc([128, 4], F32)
    onesrow = ar.alloc([1, 128], F32)
    RESIDENT = ar.mark()

    def C(name):
        return consts[:, CN[name]:CN[name] + 1]

    def fp32_transpose(srcap, np_, nf, dstap, scr_f32, scr_bf, psT3, waits):
        ev_ad = None
        for part in range(3):
            if part == 0:
                e_c = sc.add("vector", lambda h: h.tensor_copy(
                    out=scr_bf, in_=srcap), waits=waits)
                e_d = sc.add("vector", lambda h: h.tensor_tensor(
                    out=scr_f32, in0=srcap, in1=scr_bf, op=ALU.subtract))
            else:
                e_c = sc.add("vector", lambda h: h.tensor_copy(
                    out=scr_bf, in_=scr_f32), waits=[ev_ad] if ev_ad else [])
                e_d = sc.add("vector", lambda h: h.tensor_tensor(
                    out=scr_f32, in0=scr_f32, in1=scr_bf, op=ALU.subtract))
            e_t = sc.add("tensor", lambda h: h.transpose(
                out=psT3[0:nf, 0:np_], in_=scr_bf, identity=ident[:]),
                waits=[e_c, e_ident])
            if part == 0:
                ev_ad = sc.add("scalar", lambda h: h.copy(
                    out=dstap, in_=psT3[0:nf, 0:np_]), waits=[e_t])
            else:
                ev_ad = sc.add("vector", lambda h: h.tensor_tensor(
                    out=dstap, in0=dstap, in1=psT3[0:nf, 0:np_], op=ALU.add),
                    waits=[e_t, ev_ad])
        return ev_ad

    e_const = sc.add("sync", lambda h: h.dma_start(out=consts[:], in_=constin[:]), dma=True)
    e_ident = sc.add("sync", lambda h: h.dma_start(out=ident[:], in_=identin[:]), dma=True)
    e_id16 = sc.add("sync", lambda h: h.dma_start(out=ident16[:], in_=ident16in[:]), dma=True)
    e_cw = sc.add("sync", lambda h: h.dma_start(out=cw[:], in_=cwt[:]), dma=True)
    e_cb = sc.add("sync", lambda h: h.dma_start(out=cb[:], in_=cbt[:]), dma=True)
    e_ones = sc.add("vector", lambda h: h.memset(onesrow[:], 1.0))

    def load_x(b, waits=()):
        ev = None
        for i in range(NTC):
            ev = sc.add("gpsimd", lambda h, i=i, b=b: h.dma_start(
                out=xt[:, i * D:(i + 1) * D],
                in_=x_in[b, i * 128:(i + 1) * 128, :]),
                waits=waits if i == 0 else (), dma=True)
        return ev

    e_xload = load_x(0)

    m0 = ar.mark()
    ps0 = psopen()
    tvec = ar.alloc([1, T], F32)
    iv1 = ar.alloc([1, 32], F32)
    iv2 = ar.alloc([1, 32], F32)
    ftab = ar.alloc([32, T], F32)
    rtab = ar.alloc([32, T], F32)
    ntab = ar.alloc([32, T], F32)
    gtab = ar.alloc([32, T], F32)
    ntabi = ar.alloc([32, T], I32)
    sinM = ar.alloc([32, T], F32)
    cosM = ar.alloc([32, T], F32)
    psF = psum(ps0, [32, 512], F32)

    e_tv = sc.add("gpsimd", lambda h: h.iota(
        tvec[:], pattern=[[1, T]], base=0, channel_multiplier=0,
        allow_small_or_imprecise_dtypes=True))
    e_i1 = sc.add("gpsimd", lambda h: h.iota(
        iv1[:], pattern=[[1, 32]], base=0, channel_multiplier=0,
        allow_small_or_imprecise_dtypes=True))
    e_i2 = sc.add("gpsimd", lambda h: h.iota(
        iv2[:], pattern=[[1, 32]], base=0, channel_multiplier=0,
        allow_small_or_imprecise_dtypes=True))
    sc.add("vector", lambda h: h.tensor_scalar(
        out=iv1[:], in0=iv1[:], scalar1=-_LN_HI / 32.0, scalar2=None, op0=ALU.mult),
        waits=[e_i1])
    e_v2 = sc.add("vector", lambda h: h.scalar_tensor_tensor(
        out=iv1[:], in0=iv2[:], scalar=-_LN_LO / 32.0, op0=ALU.mult, op1=ALU.add,
        in1=iv1[:]), waits=[e_i2])
    e_exp = sc.add("scalar", lambda h: h.activation(
        out=iv1[:], in_=iv1[:], func=AF.Exp), waits=[e_v2])
    e_cp = 0
    for tg in range(T // 512):
        wt = [e_exp, e_tv]
        if tg:
            wt.append(e_cp)
        e_mmf = sc.add("tensor", lambda h, tg=tg: h.matmul(
            out=psF[:, :], lhsT=iv1[:], rhs=tvec[:, tg * 512:(tg + 1) * 512],
            start=True, stop=True), waits=wt)
        e_cp = sc.add("vector", lambda h, tg=tg: h.tensor_copy(
            out=ftab[:, tg * 512:(tg + 1) * 512], in_=psF[:, :]),
            waits=[e_mmf])
    sc.add("vector", lambda h: h.tensor_scalar(
        out=ntab[:], in0=ftab[:], scalar1=1.0 / _TWO_PI, scalar2=None, op0=ALU.mult))
    sc.add("vector", lambda h: h.tensor_scalar(
        out=ntabi[:], in0=ntab[:], scalar1=0.0, scalar2=None, op0=ALU.add))
    sc.add("vector", lambda h: h.tensor_copy(out=gtab[:], in_=ntabi[:]))
    sc.add("vector", lambda h: h.tensor_tensor(
        out=ntab[:], in0=gtab[:], in1=ntab[:], op=ALU.is_gt))
    sc.add("vector", lambda h: h.tensor_tensor(
        out=ntab[:], in0=gtab[:], in1=ntab[:], op=ALU.subtract))
    for c in (_C1, _C2, _C3):
        sc.add("vector", lambda h, c=c: h.scalar_tensor_tensor(
            out=rtab[:] if c != _C1 else rtab[:], in0=ntab[:], scalar=-c,
            op0=ALU.mult, op1=ALU.add, in1=ftab[:] if c == _C1 else rtab[:]))
    sc.add("vector", lambda h: h.tensor_scalar(
        out=gtab[:], in0=rtab[:], scalar1=_PI_F32, scalar2=None, op0=ALU.is_gt))
    for c in (_C1, _C2, _C3):
        sc.add("vector", lambda h, c=c: h.scalar_tensor_tensor(
            out=rtab[:], in0=gtab[:], scalar=-c, op0=ALU.mult, op1=ALU.add,
            in1=rtab[:]))
    e_cl = sc.add("vector", lambda h: h.tensor_scalar(
        out=rtab[:], in0=rtab[:], scalar1=_PI_F32, scalar2=-_PI_F32,
        op0=ALU.min, op1=ALU.max))
    e_sin = sc.add("scalar", lambda h: h.activation(
        out=sinM[:], in_=rtab[:], func=AF.Sin), waits=[e_cl])
    e_abs = sc.add("scalar", lambda h: h.activation(
        out=rtab[:], in_=rtab[:], func=AF.Abs))
    sc.add("vector", lambda h: h.tensor_scalar(
        out=rtab[:], in0=rtab[:], scalar1=-1.0, scalar2=_PIH_HI,
        op0=ALU.mult, op1=ALU.add), waits=[e_abs])
    e_r2 = sc.add("vector", lambda h: h.tensor_scalar(
        out=rtab[:], in0=rtab[:], scalar1=_PIH_LO, scalar2=None, op0=ALU.add))
    e_cos = sc.add("scalar", lambda h: h.activation(
        out=cosM[:], in_=rtab[:], func=AF.Sin), waits=[e_r2])
    e_ropew1 = sc.add("gpsimd", lambda h: h.dma_start(
        out=roped[0], in_=sinM[:]), waits=[e_sin], dma=True)
    e_ropew2 = sc.add("gpsimd", lambda h: h.dma_start(
        out=roped[1], in_=cosM[:]), waits=[e_cos], dma=True)
    psclose()

    ps0 = psopen()
    wbuf = [ar.alloc([128, HID], F32) for _ in range(2)]
    wabs = ar.alloc([128, HID], F32)
    wsgn = ar.alloc([128, HID], BF16)
    wqb = ar.alloc([128, HID], BF16)
    wsum = ar.alloc([128, 8], F32)
    wtot = ar.alloc([128, 1], F32)
    w11 = ar.alloc([1, 2], F32)
    psW = psum(ps0, [1, 2], F32)
    psW2 = psum(ps0, [128, 2], F32)

    def wquant(load_chunk, store_chunk, nch, ncol, nelem, sw_col, invsw_col):
        ev_red = []
        e_l = 0
        for c in range(nch):
            buf = wbuf[c % 2]
            wprev = [ev_red[c - 2]] if c >= 2 else []
            e_l = load_chunk(c, buf, wprev)
            ev_red.append(sc.add("vector", lambda h, c=c, buf=buf: h.tensor_reduce(
                out=wsum[:, c:c + 1], in_=buf[:, 0:ncol], axis=AX.X,
                op=ALU.add, apply_absolute_value=True), waits=[e_l]))
        e_t = sc.add("vector", lambda h: h.tensor_reduce(
            out=wtot[:], in_=wsum[:, 0:nch], axis=AX.X, op=ALU.add))
        e_mm = sc.add("tensor", lambda h: h.matmul(
            out=psW[0:1, 0:1], lhsT=wtot[:], rhs=C("one"), start=True, stop=True),
            waits=[e_t, e_const])
        sc.add("vector", lambda h: h.tensor_copy(out=w11[0:1, 0:1], in_=psW[0:1, 0:1]),
               waits=[e_mm])
        sc.add("vector", lambda h: h.tensor_scalar(
            out=w11[0:1, 0:1], in0=w11[0:1, 0:1], scalar1=1.0 / nelem,
            scalar2=1e-5, op0=ALU.mult, op1=ALU.max))
        e_c3 = sc.add("vector", lambda h: h.reciprocal(
            out=w11[0:1, 1:2], in_=w11[0:1, 0:1]))
        e_bm = sc.add("tensor", lambda h: h.matmul(
            out=psW2[:, :], lhsT=onesrow[:], rhs=w11[:], start=True, stop=True),
            waits=[e_c3, e_ones])
        e_bc = sc.add("vector", lambda h, sw_col=sw_col, invsw_col=invsw_col: h.tensor_copy(
            out=smallv[:, invsw_col:invsw_col + 1], in_=psW2[:, 0:1]),
            waits=[e_bm])
        e_bc = sc.add("vector", lambda h, sw_col=sw_col: h.tensor_copy(
            out=smallv[:, sw_col:sw_col + 1], in_=psW2[:, 1:2]))
        ev_q = []
        e_st = 0
        for c in range(nch):
            buf = wbuf[c % 2]
            wprev = [ev_q[c - 2]] if c >= 2 else [ev_red[-1]]
            e_l = load_chunk(c, buf, wprev)
            e_a = sc.add("scalar", lambda h, buf=buf: h.activation(
                out=wabs[:, 0:ncol], in_=buf[:, 0:ncol], func=AF.Abs,
                scale=smallv[:, sw_col:sw_col + 1]),
                waits=[e_l, e_bc])
            e_s = sc.add("scalar", lambda h, buf=buf: h.activation(
                out=wsgn[:, 0:ncol], in_=buf[:, 0:ncol], func=AF.Sign))
            sc.add("vector", lambda h: h.tensor_scalar(
                out=wabs[:, 0:ncol], in0=wabs[:, 0:ncol], scalar1=0.5,
                scalar2=None, op0=ALU.is_ge), waits=[e_a])
            e_q = sc.add("vector", lambda h: h.tensor_tensor(
                out=wqb[:, 0:ncol], in0=wabs[:, 0:ncol], in1=wsgn[:, 0:ncol],
                op=ALU.mult), waits=[e_s])
            ev_q.append(e_q)
            e_st = store_chunk(c, e_q)
        return e_st

    def w1_load(c, buf, wprev):
        return sc.add("sync", lambda h, c=c, buf=buf: h.dma_start(
            out=buf[:, 0:HID], in_=w1t_d[c * 128:(c + 1) * 128, :]),
            waits=wprev, dma=True)

    def w1_store(c, e_q):
        return sc.add("sync", lambda h, c=c: h.dma_start(
            out=wq1d[c * 128:(c + 1) * 128, :], in_=wqb[:, 0:HID]),
            waits=[e_q], dma=True)

    e_wq1 = wquant(w1_load, w1_store, NDC, HID, D * HID, sw_col=0, invsw_col=1)

    def w2_load(c, buf, wprev):
        ev = None
        for j in range(4):
            ev = sc.add("sync", lambda h, c=c, j=j, buf=buf: h.dma_start(
                out=buf[:, j * D:(j + 1) * D],
                in_=w2t_d[c * 512 + j * 128:c * 512 + (j + 1) * 128, :]),
                waits=wprev if j == 0 else (), dma=True)
        return ev

    def w2_store(c, e_q):
        ev = None
        for j in range(4):
            ev = sc.add("sync", lambda h, c=c, j=j: h.dma_start(
                out=wq2d[c * 512 + j * 128:c * 512 + (j + 1) * 128, :],
                in_=wqb[:, j * D:(j + 1) * D]),
                waits=[e_q] if j == 0 else (), dma=True)
        return ev

    e_wq2 = wquant(w2_load, w2_store, 8, HID, D * HID, sw_col=2, invsw_col=3)

    def wcast(src, dst, nch, ncol):
        ev = None
        evq_h = []
        for c in range(nch):
            buf = wbuf[c % 2]
            wprev = [evq_h[c - 2]] if c >= 2 else []
            e_l = sc.add("sync", lambda h, c=c, buf=buf: h.dma_start(
                out=buf[:, 0:ncol], in_=src[c * 128:(c + 1) * 128, :]),
                waits=wprev, dma=True)
            e_q = sc.add("vector", lambda h, buf=buf: h.tensor_copy(
                out=wqb[:, 0:ncol], in_=buf[:, 0:ncol]), waits=[e_l])
            evq_h.append(e_q)
            ev = sc.add("sync", lambda h, c=c: h.dma_start(
                out=dst[c * 128:(c + 1) * 128, :], in_=wqb[:, 0:ncol]),
                waits=[e_q], dma=True)
        return ev

    e_wqkv = wcast(w_qkv, wqkvd, NDC, 3 * D)
    e_wo = wcast(w_o, wod, NDC, D)
    psclose()
    ar.reset(m0)

    sc.barrier()

    def lnq(want_xq, e_xrdy, outF):
        mk = ar.mark()
        stk = psopen()
        red1 = ar.alloc([128, NTC], F32)
        red2 = ar.alloc([128, NTC], F32)
        rstd = ar.alloc([128, NTC], F32)
        nmr = ar.alloc([128, NTC], F32)
        sxc = ar.alloc([128, NTC], F32)
        invs = ar.alloc([128, NTC], F32)
        sqs = ar.alloc([128, D], F32)
        xhat = [ar.alloc([128, D], F32) for _ in range(2)]
        i32s = [ar.alloc([128, D], I32) for _ in range(2)]
        ihalf = [ar.alloc([128, D], BF16) for _ in range(2)]
        fixb = ar.alloc([128, D], BF16)
        xqT = [ar.alloc([128, D], BF16) for _ in range(2)]
        invsbf = ar.alloc([128, NTC], BF16)
        invsd = ar.alloc([128, NTC], F32)
        statT = ar.alloc([16, 128], F32)
        lnrow = ar.alloc([1, T], F32) if want_xq else None
        psT = [psum(stk, [128, 512], BF16) for _ in range(2)]
        psG = psum(stk, [1, NTC * 128], F32) if want_xq else None

        for i in range(NTC):
            xi = xt[:, i * D:(i + 1) * D]
            sc.add("vector", lambda h, xi=xi, i=i: h.tensor_reduce(
                out=red1[:, i:i + 1], in_=xi, axis=AX.X, op=ALU.add),
                waits=[e_xrdy] if i == 0 else [])
            sc.add("scalar", lambda h, xi=xi, i=i: h.activation(
                out=sqs[:], in_=xi, func=AF.Square, accum_out=red2[:, i:i + 1]),
                waits=[e_xrdy] if i == 0 else [])
        e_sq = sc.evt("scalar")
        sc.add("vector", lambda h: h.tensor_scalar(
            out=red1[:], in0=red1[:], scalar1=1.0 / D, scalar2=None, op0=ALU.mult),
            waits=[e_sq])
        sc.add("vector", lambda h: h.tensor_scalar(
            out=red2[:], in0=red2[:], scalar1=1.0 / D, scalar2=None, op0=ALU.mult))
        sc.add("vector", lambda h: h.tensor_tensor(
            out=rstd[:], in0=red1[:], in1=red1[:], op=ALU.mult))
        sc.add("vector", lambda h: h.tensor_tensor(
            out=red2[:], in0=red2[:], in1=rstd[:], op=ALU.subtract))
        e_ve = sc.add("vector", lambda h: h.tensor_scalar(
            out=red2[:], in0=red2[:], scalar1=1e-5, scalar2=None, op0=ALU.add))
        e_sd = sc.add("scalar", lambda h: h.activation(
            out=rstd[:], in_=red2[:], func=AF.Sqrt), waits=[e_ve])
        sc.add("vector", lambda h: h.reciprocal(out=rstd[:], in_=rstd[:]),
               waits=[e_sd])
        sc.add("vector", lambda h: h.tensor_tensor(
            out=nmr[:], in0=rstd[:], in1=rstd[:], op=ALU.mult))
        sc.add("vector", lambda h: h.tensor_tensor(
            out=nmr[:], in0=nmr[:], in1=red2[:], op=ALU.mult))
        sc.add("vector", lambda h: h.tensor_scalar(
            out=nmr[:], in0=nmr[:], scalar1=-0.5, scalar2=1.5,
            op0=ALU.mult, op1=ALU.add))
        sc.add("vector", lambda h: h.tensor_tensor(
            out=rstd[:], in0=rstd[:], in1=nmr[:], op=ALU.mult))
        sc.add("vector", lambda h: h.tensor_tensor(
            out=nmr[:], in0=red1[:], in1=rstd[:], op=ALU.mult))
        e_b = sc.add("vector", lambda h: h.tensor_scalar(
            out=nmr[:], in0=nmr[:], scalar1=-1.0, scalar2=None, op0=ALU.mult))

        e_tr = []
        e_q_hist = {}
        e_i32_hist = {}
        e_trT_hist = {}
        for i in range(NTC):
            xi = xt[:, i * D:(i + 1) * D]
            xh = xhat[i % 2]
            i32 = i32s[i % 2]
            ih = ihalf[i % 2]
            xq = xqT[i % 2]
            if want_xq:
                wt = [e_b] if i == 0 else []
                if i >= 2:
                    wt.append(e_i32_hist[i - 2])
                sc.add("vector", lambda h, xi=xi, xh=xh, i=i: h.tensor_scalar(
                    out=xh[:], in0=xi, scalar1=rstd[:, i:i + 1],
                    scalar2=nmr[:, i:i + 1], op0=ALU.mult, op1=ALU.add), waits=wt)
                sc.add("vector", lambda h, xh=xh, i=i: h.tensor_reduce(
                    out=sxc[:, i:i + 1], in_=xh[:], axis=AX.X, op=ALU.max,
                    apply_absolute_value=True))
                sc.add("vector", lambda h, i=i: h.tensor_scalar(
                    out=sxc[:, i:i + 1], in0=sxc[:, i:i + 1], scalar1=1e-5,
                    scalar2=None, op0=ALU.max))
                sc.add("vector", lambda h, i=i: h.tensor_scalar(
                    out=invs[:, i:i + 1], in0=sxc[:, i:i + 1],
                    scalar1=smallv[:, 1:2], scalar2=1.0 / 127.0,
                    op0=ALU.mult, op1=ALU.mult))
                sc.add("vector", lambda h, i=i: h.reciprocal(
                    out=sxc[:, i:i + 1], in_=sxc[:, i:i + 1]))
                e_sx = sc.add("vector", lambda h, i=i: h.tensor_scalar(
                    out=sxc[:, i:i + 1], in0=sxc[:, i:i + 1], scalar1=127.0,
                    scalar2=None, op0=ALU.mult))
                wt2 = [e_sx]
                if i >= 2:
                    wt2.append(e_q_hist[i - 2])
                e_i32_hist[i] = sc.add("scalar", lambda h, xh=xh, i32=i32, i=i: h.activation(
                    out=i32[:], in_=xh[:], func=AF.Identity,
                    scale=sxc[:, i:i + 1], bias=C("half")), waits=wt2)
                e_ih = sc.add("scalar", lambda h, i32=i32, ih=ih: h.activation(
                    out=ih[:], in_=i32[:], func=AF.Identity, bias=C("neghalf")))
                sc.add("vector", lambda h, xh=xh, ih=ih, i=i: h.scalar_tensor_tensor(
                    out=fixb[:], in0=xh[:], scalar=sxc[:, i:i + 1], op0=ALU.mult,
                    op1=ALU.is_lt, in1=ih[:]), waits=[e_ih])
                wtq = []
                if i >= 2:
                    wtq.append(e_trT_hist[i - 2])
                e_src = sc.add("vector", lambda h, ih=ih, xq=xq: h.scalar_tensor_tensor(
                    out=xq[:], in0=ih[:], scalar=0.5, op0=ALU.add,
                    op1=ALU.subtract, in1=fixb[:]), waits=wtq)
                e_q_hist[i] = e_src
            else:
                wt = [e_b] if i == 0 else []
                if i >= 2:
                    wt.append(e_trT_hist[i - 2])
                e_src = sc.add("vector", lambda h, xi=xi, xq=xq, i=i: h.tensor_scalar(
                    out=xq[:], in0=xi, scalar1=rstd[:, i:i + 1],
                    scalar2=nmr[:, i:i + 1], op0=ALU.mult, op1=ALU.add), waits=wt)
                e_q_hist[i] = e_src
            for k in range(NDC):
                n = i * NDC + k
                ps = psT[n % 2]
                wt3 = [e_src, e_ident]
                if n >= 2:
                    wt3.append(e_tr[n - 2])
                e_t = sc.add("tensor", lambda h, xq=xq, k=k, ps=ps: h.transpose(
                    out=ps[:, 0:128], in_=xq[:, k * 128:(k + 1) * 128],
                    identity=ident[:]), waits=wt3)
                e_trT_hist[i] = e_t
                e_tr.append(sc.add("scalar", lambda h, i=i, k=k, ps=ps: h.copy(
                    out=outF[k][:, i * 128:(i + 1) * 128], in_=ps[:, 0:128]),
                    waits=[e_t]))
        e_s_done = sc.evt("scalar")
        e_statw = None
        if want_xq:
            e_tT = fp32_transpose(invs[:], 128, NTC, statT[0:NTC, :],
                                  invsd[:], invsbf[:], psT[0],
                                  [sc.evt("vector"), e_tr[-1]])
            e_gm = 0
            for i in range(NTC):
                e_gm = sc.add("tensor", lambda h, i=i: h.matmul(
                    out=psG[0:1, i * 128:(i + 1) * 128],
                    lhsT=ident16[0:NTC, i:i + 1], rhs=statT[0:NTC, :],
                    start=True, stop=True),
                    waits=[e_tT, e_id16] if i == 0 else [])
            e_ev = sc.add("scalar", lambda h: h.copy(
                out=lnrow[:], in_=psG[:]), waits=[e_gm])
            e_statw = sc.add("gpsimd", lambda h: h.dma_start(
                out=statd.rearrange("(a t) -> a t", a=1)[:, 0:T], in_=lnrow[:]),
                waits=[e_ev], dma=True)
        sc.barrier()
        psclose()
        ar.reset(mk)
        return e_s_done, e_statw

    def ffn(e_xrdy):
        mkF = ar.mark()
        invs2c = ar.alloc([128, NTC], F32)
        mkB = ar.mark()
        xqF = [ar.alloc([128, T], BF16) for _ in range(NDC)]
        e_lnq_s, e_statw = lnq(True, e_xrdy, xqF)

        mkA = ar.mark()
        stkA = psopen()
        s1row = ar.alloc([1, 258], F32)
        wq1sb = [ar.alloc([128, HID], BF16) for _ in range(NDC)]
        srepg = ar.alloc([128, 258], F32)
        hraw = ar.alloc([128, 258], F32)
        cbuf = ar.alloc([128, 256], F32)
        h2 = [ar.alloc([128, 256], F32) for _ in range(NHC)]
        racc = ar.alloc([128, 256], F32)
        raccT = ar.alloc([128, 256], F32)
        raccT_off = ar.last
        sxr = ar.alloc([128, 256], F32)
        splf = ar.alloc([128, 128], F32)
        splf_off = ar.last
        splb = ar.alloc([128, 128], BF16)
        amax2 = ar.alloc([128, 2], F32)
        sx2t = ar.alloc([128, 2], F32)
        statT2 = ar.alloc([2, 128], F32)
        sx2row = ar.alloc([1, 256], F32)
        zb = ar.alloc([128, 256], F32, at=raccT_off)
        i32g = ar.alloc([128, 256], I32)
        i32g_off = ar.last
        rmin = ar.alloc([128, 256], F32, at=i32g_off)
        ihg = ar.alloc([128, 256], BF16)
        fxg = ar.alloc([128, 256], BF16, at=splf_off)
        hqs_strip = [ar.alloc([128, 256], BF16) for _ in range(2)]
        psH = [psum(stkA, [128, 512], F32) for _ in range(2)]
        psS = psum(stkA, [128, 512], F32)
        psT3 = psum(stkA, [128, 256], BF16)

        e_w1l = 0
        for k in range(NDC):
            e_w1l = sc.add("sync", lambda h, k=k: h.dma_start(
                out=wq1sb[k][:], in_=wq1d[k * 128:(k + 1) * 128, :]),
                waits=[e_wq1] if k == 0 else (), dma=True)

        e_spill = []
        e_se_prev = 0
        e_sil_prev = 0
        e_c3_prev = 0
        e_fx_prev = 0
        e_zb_guard = []
        e_deq = {}
        e_i2w = 0
        for g in range(NGA):
            lo = max(g * GA - 1, 0)
            hi_ = min(g * GA + GA + 1, T)
            w = hi_ - lo
            coff = 1 if g == 0 else 0
            wt = [e_statw]
            if e_se_prev:
                wt.append(e_se_prev)
            e_r1 = sc.add("sync", lambda h, lo=lo, w=w: h.dma_start(
                out=s1row[0:1, 0:w],
                in_=statd.rearrange("(a t) -> a t", a=1)[:, lo:lo + w]),
                waits=wt, dma=True)
            e_bm = sc.add("tensor", lambda h, w=w: h.matmul(
                out=psS[:, 0:w], lhsT=onesrow[:], rhs=s1row[0:1, 0:w],
                start=True, stop=True),
                waits=[e_r1, e_ones] + ([e_se_prev] if e_se_prev else []))
            wt_se = [e_bm]
            if g > 0:
                wt_se.append(e_deq[(g - 1, NHC - 1)])
            e_se = sc.add("scalar", lambda h, w=w, coff=coff: h.copy(
                out=srepg[:, coff:coff + w], in_=psS[:, 0:w]), waits=wt_se)
            e_se_prev = e_se
            e_z = []
            if g == 0:
                e_z.append(sc.add("vector", lambda h: h.memset(hraw[:, 0:1], 0.0)))
            if g == NGA - 1:
                e_z.append(sc.add("vector", lambda h: h.memset(hraw[:, 257:258], 0.0)))
            e_r = 0
            for m in range(NHC):
                ps = psH[m % 2]
                wt1 = [e_w1l, e_lnq_s]
                if (g, m) in e_deq or ((g * NHC + m) >= 2):
                    n = g * NHC + m - 2
                    wt1.append(e_deq[(n // NHC, n % NHC)])
                e_mm = 0
                for k in range(NDC):
                    e_mm = sc.add("tensor", lambda h, k=k, m=m, lo=lo, w=w, ps=ps: h.matmul(
                        out=ps[:, 0:w],
                        lhsT=wq1sb[k][:, m * 128:(m + 1) * 128],
                        rhs=xqF[k][:, lo:lo + w],
                        start=(k == 0), stop=(k == NDC - 1)),
                        waits=wt1 if k == 0 else [])
                wt_dq = [e_mm, e_se]
                if e_c3_prev:
                    wt_dq.append(e_c3_prev)
                e_dq = sc.add("vector", lambda h, w=w, coff=coff, ps=ps: h.tensor_tensor(
                    out=hraw[:, coff:coff + w], in0=ps[:, 0:w],
                    in1=srepg[:, coff:coff + w], op=ALU.mult),
                    waits=wt_dq)
                e_deq[(g, m)] = e_dq
                wt_c1 = [e_dq, e_cb]
                if e_sil_prev:
                    wt_c1.append(e_sil_prev)
                e_c1 = sc.add("scalar", lambda h, m=m: h.activation(
                    out=cbuf[:], in_=hraw[:, 0:256], func=AF.Identity,
                    scale=cw[:, 3 * m:3 * m + 1], bias=cb[:, m:m + 1]),
                    waits=wt_c1)
                e_c2 = sc.add("vector", lambda h, m=m: h.scalar_tensor_tensor(
                    out=cbuf[:], in0=hraw[:, 1:257], scalar=cw[:, 3 * m + 1:3 * m + 2],
                    op0=ALU.mult, op1=ALU.add, in1=cbuf[:]),
                    waits=[e_c1])
                e_c3 = sc.add("vector", lambda h, m=m: h.scalar_tensor_tensor(
                    out=cbuf[:], in0=hraw[:, 2:258], scalar=cw[:, 3 * m + 2:3 * m + 3],
                    op0=ALU.mult, op1=ALU.add, in1=cbuf[:]),
                    waits=[e_c2])
                e_c3_prev = e_c3
                e_sil = sc.add("scalar", lambda h, m=m: h.activation(
                    out=h2[m][:], in_=cbuf[:], func=AF.Silu),
                    waits=[e_c3])
                e_sil_prev = e_sil
                if m == 0:
                    e_r = sc.add("vector", lambda h: h.tensor_copy(
                        out=racc[:], in_=h2[0][:]), waits=[e_sil])
                    e_rG = sc.add("vector", lambda h: h.tensor_copy(
                        out=rmin[:], in_=h2[0][:]), waits=[e_sil])
                else:
                    e_r = sc.add("vector", lambda h, m=m: h.tensor_tensor(
                        out=racc[:], in0=racc[:], in1=h2[m][:], op=ALU.max),
                        waits=[e_sil])
                    e_rG = sc.add("vector", lambda h, m=m: h.tensor_tensor(
                        out=rmin[:], in0=rmin[:], in1=h2[m][:], op=ALU.min),
                        waits=[e_sil])
            e_tT = 0
            for blk in range(2):
                e_tT = fp32_transpose(racc[:, blk * 128:(blk + 1) * 128], 128, 128,
                                      raccT[:, blk * 128:(blk + 1) * 128],
                                      splf[:], splb[:], psT3,
                                      [e_r] + ([e_tT] if e_tT else [])
                                      + (e_zb_guard if blk == 0 else []))
                sc.add("vector", lambda h, blk=blk: h.tensor_reduce(
                    out=amax2[:, blk:blk + 1],
                    in_=raccT[:, blk * 128:(blk + 1) * 128], axis=AX.X,
                    op=ALU.max), waits=[e_tT])
            for blk in range(2):
                e_tT = fp32_transpose(rmin[:, blk * 128:(blk + 1) * 128], 128, 128,
                                      raccT[:, blk * 128:(blk + 1) * 128],
                                      splf[:], splb[:], psT3,
                                      [e_rG, e_tT])
                sc.add("vector", lambda h, blk=blk: h.tensor_reduce(
                    out=sx2t[:, blk:blk + 1],
                    in_=raccT[:, blk * 128:(blk + 1) * 128], axis=AX.X,
                    op=ALU.min), waits=[e_tT])
            sc.add("vector", lambda h: h.tensor_scalar(
                out=sx2t[:], in0=sx2t[:], scalar1=-1.0, scalar2=None,
                op0=ALU.mult))
            sc.add("vector", lambda h: h.tensor_tensor(
                out=amax2[:], in0=amax2[:], in1=sx2t[:], op=ALU.max))
            sc.add("vector", lambda h: h.tensor_scalar(
                out=amax2[:], in0=amax2[:], scalar1=1e-5, scalar2=None,
                op0=ALU.max))
            sc.add("vector", lambda h, g=g: h.tensor_scalar(
                out=invs2c[:, 2 * g:2 * g + 2], in0=amax2[:],
                scalar1=smallv[:, 3:4], scalar2=0.5 / 127.0,
                op0=ALU.mult, op1=ALU.mult))
            sc.add("vector", lambda h: h.reciprocal(out=sx2t[:], in_=amax2[:]))
            e_sx2t = sc.add("vector", lambda h: h.tensor_scalar(
                out=sx2t[:], in0=sx2t[:], scalar1=127.0, scalar2=None,
                op0=ALU.mult))
            e_tT2 = fp32_transpose(sx2t[:], 128, 2, statT2[:],
                                   splf[:, 0:2], splb[:, 0:2], psT3, [e_sx2t])
            e_g2 = 0
            for j in range(2):
                e_g2 = sc.add("tensor", lambda h, j=j: h.matmul(
                    out=psS[0:1, j * 128:(j + 1) * 128],
                    lhsT=ident16[0:2, j:j + 1], rhs=statT2[:],
                    start=True, stop=True),
                    waits=[e_tT2, e_id16, e_se] if j == 0 else [])
            e_rw = sc.add("scalar", lambda h: h.copy(
                out=sx2row[:], in_=psS[0:1, 0:256]), waits=[e_g2])
            e_bm2 = sc.add("tensor", lambda h: h.matmul(
                out=psS[:, 0:256], lhsT=onesrow[:], rhs=sx2row[:],
                start=True, stop=True), waits=[e_rw])
            e_sxr = sc.add("scalar", lambda h: h.copy(
                out=sxr[:], in_=psS[:, 0:256]), waits=[e_bm2])
            e_se_prev = e_sxr
            e_i_prev = 0
            e_hq_prev = 0
            for m in range(NHC):
                strip = hqs_strip[m % 2]
                wt4 = [e_sxr] if m == 0 else []
                if m >= 2:
                    wt4.append(e_spill[-2])
                if e_i_prev:
                    wt4.append(e_i_prev)
                if e_fx_prev:
                    wt4.append(e_fx_prev)
                e_zm = sc.add("gpsimd", lambda h, m=m: h.tensor_tensor(
                    out=zb[:], in0=h2[m][:], in1=sxr[:], op=ALU.mult), waits=wt4)
                e_i = sc.add("scalar", lambda h: h.activation(
                    out=i32g[:], in_=zb[:], func=AF.Identity, bias=C("half")),
                    waits=[e_zm] +
                          ([e_hq_prev] if e_hq_prev else []))
                e_i_prev = e_i
                e_h = sc.add("scalar", lambda h: h.activation(
                    out=ihg[:], in_=i32g[:], func=AF.Identity, bias=C("neghalf")))
                e_fx_prev = sc.add("vector", lambda h: h.tensor_tensor(
                    out=fxg[:], in0=ihg[:], in1=zb[:], op=ALU.is_gt),
                    waits=[e_h])
                e_hq = sc.add("vector", lambda h, strip=strip: h.scalar_tensor_tensor(
                    out=strip[:], in0=ihg[:], scalar=0.5, op0=ALU.add,
                    op1=ALU.subtract, in1=fxg[:]))
                e_hq_prev = e_hq
                e_spill.append(sc.add("gpsimd", lambda h, g=g, m=m, strip=strip: h.dma_start(
                    out=hqd[m, :, g * GA:(g + 1) * GA], in_=strip[:]),
                    waits=[e_hq], dma=True))
            e_zb_guard = [e_zm, e_i_prev]
        psclose()
        ar.reset(mkB)
        B = sc.barrier()

        stkB = psopen()
        wq2sb = [ar.alloc([128, D], BF16) for _ in range(NHC)]
        hqs = [ar.alloc([128, NHC * GB], BF16) for _ in range(2)]
        syc = invs2c
        psY = [psum(stkB, [128, D], F32) for _ in range(2)]

        e_w2l = 0
        for m in range(NHC):
            e_w2l = sc.add("sync", lambda h, m=m: h.dma_start(
                out=wq2sb[m][:], in_=wq2d[m * 128:(m + 1) * 128, :]),
                waits=(B + [e_wq2]) if m == 0 else (), dma=True)

        e_res_hist = []
        e_gdone = []
        e_mm = 0
        for g in range(NGB):
            buf = hqs[g % 2]
            wt = list(B)
            if g >= 2:
                wt.append(e_gdone[g - 2])
            e_hl = 0
            for m in range(NHC):
                e_hl = sc.add("sync", lambda h, g=g, m=m, buf=buf: h.dma_start(
                    out=buf[:, m * GB:(m + 1) * GB],
                    in_=hqd[m, :, g * GB:(g + 1) * GB]),
                    waits=wt if m == 0 else (), dma=True)
            for ts in range(4):
                tch = g * 4 + ts
                ps = psY[ts % 2]
                wt2 = [e_hl, e_w2l]
                if len(e_res_hist) >= 2:
                    wt2.append(e_res_hist[-2])
                for m in range(NHC):
                    for eh in range(2):
                        e_mm = sc.add("tensor", lambda h, m=m, ts=ts, eh=eh, ps=ps, buf=buf: h.matmul(
                            out=ps[:, eh * 512:(eh + 1) * 512],
                            lhsT=buf[:, m * GB + ts * 128:m * GB + (ts + 1) * 128],
                            rhs=wq2sb[m][:, eh * 512:(eh + 1) * 512],
                            start=(m == 0), stop=(m == NHC - 1)),
                            waits=wt2 if (m == 0 and eh == 0) else [])
                e_res_hist.append(sc.add("vector", lambda h, tch=tch, ps=ps: h.scalar_tensor_tensor(
                    out=xt[:, tch * D:(tch + 1) * D], in0=ps[:, :],
                    scalar=syc[:, tch:tch + 1], op0=ALU.mult, op1=ALU.add,
                    in1=xt[:, tch * D:(tch + 1) * D]),
                    waits=[e_mm]))
            e_gdone.append(e_mm)
        psclose()
        ar.reset(mkF)
        sc.barrier()
        return sc.evt("vector")

    def att(e_xrdy):
        mkA = ar.mark()
        xhF = [ar.alloc([128, T], BF16) for _ in range(NDC)]
        e_lnq_s, _ = lnq(False, e_xrdy, xhF)

        stk = psopen()
        obuf = [ar.alloc([128, T], BF16) for _ in range(NDC)]
        mk2 = ar.mark()
        crep = ar.alloc([128, T], F32)
        srep = ar.alloc([128, T], F32)
        wkv = [ar.alloc([128, 3 * 256], BF16) for _ in range(NDC)]
        qkF = ar.alloc([128, T], F32)
        qswp = ar.alloc([128, T], F32)
        qhB = [ar.alloc([128, T], BF16) for _ in range(2)]
        khB = [ar.alloc([128, T], BF16) for _ in range(2)]
        vtok = ar.alloc([128, NKC * 132], BF16)
        expb = [ar.alloc([128, QHW], BF16) for _ in range(2)]
        drow = ar.alloc([1, QHW], F32)
        drep = ar.alloc([64, QHW], F32)
        rmst = qswp[0:32, :]
        psS_ = [psum(stk, [128, QHW], F32) for _ in range(2)]
        psO = [psum(stk, [128, QHW], F32) for _ in range(2)]

        e_rs = sc.add("sync", lambda h: h.dma_start(
            out=rmst, in_=roped[0]), waits=[e_ropew1], dma=True)
        for q in range(4):
            sc.add("vector", lambda h, q=q: h.tensor_copy(
                out=srep[32 * q:32 * (q + 1), :], in_=rmst),
                waits=[e_rs] if q == 0 else [])
        e_rc = sc.add("sync", lambda h: h.dma_start(
            out=rmst, in_=roped[1]),
            waits=[e_ropew2, sc.evt("vector")], dma=True)
        for q in range(4):
            sc.add("vector", lambda h, q=q: h.tensor_copy(
                out=crep[32 * q:32 * (q + 1), :], in_=rmst),
                waits=[e_rc] if q == 0 else [])
        sc.add("vector", lambda h: h.tensor_scalar(
            out=srep[0:32, :], in0=srep[0:32, :], scalar1=-1.0, scalar2=None,
            op0=ALU.mult))
        sc.add("vector", lambda h: h.tensor_scalar(
            out=srep[64:96, :], in0=srep[64:96, :], scalar1=-1.0, scalar2=None,
            op0=ALU.mult))
        e_tab = sc.evt("vector")
        e_von = sc.add("vector", lambda h: h.memset(
            vtok.ap().rearrange("p (k c) -> p k c", k=NKC)[:, :, 64:65], 1.0))
        e_von2 = sc.add("vector", lambda h: h.memset(
            vtok.ap().rearrange("p (k c) -> p k c", k=NKC)[:, :, 130:131], 1.0))

        e_on = 0
        for hg in range(NHG):
            Bh = sc.barrier()
            e_wl = 0
            for k in range(NDC):
                for part in range(3):
                    e_wl = sc.add("sync", lambda h, k=k, part=part, hg=hg: h.dma_start(
                        out=wkv[k][:, part * 256:(part + 1) * 256],
                        in_=wqkvd[k * 128:(k + 1) * 128,
                                  part * D + hg * 256:part * D + (hg + 1) * 256]),
                        waits=(Bh + [e_wqkv]) if (k == 0 and part == 0) else (),
                        dma=True)
            e_rope = 0
            e_s = 0
            for part in range(2):
                dstl = qhB if part == 0 else khB
                for mc in range(2):
                    e_ev = 0
                    for tg in range(T // 512):
                        ps = psS_[tg % 2]
                        wt = [e_wl, e_lnq_s] + Bh
                        if tg >= 2:
                            wt.append(e_ev)
                        if tg == 0 and e_rope:
                            wt.append(e_rope)
                            wt.append(e_s)
                        e_mm = 0
                        for k in range(NDC):
                            e_mm = sc.add("tensor", lambda h, k=k, part=part, mc=mc, tg=tg, ps=ps: h.matmul(
                                out=ps[:, 0:512],
                                lhsT=wkv[k][:, part * 256 + mc * 128:part * 256 + (mc + 1) * 128],
                                rhs=xhF[k][:, tg * 512:(tg + 1) * 512],
                                start=(k == 0), stop=(k == NDC - 1)),
                                waits=wt if k == 0 else [])
                        e_ev = sc.add("scalar", lambda h, tg=tg, ps=ps: h.copy(
                            out=qkF[:, tg * 512:(tg + 1) * 512], in_=ps[:, 0:512]),
                            waits=[e_mm])
                    for (a, b2) in ((0, 32), (32, 0), (64, 96), (96, 64)):
                        e_s = sc.add("gpsimd", lambda h, a=a, b2=b2: h.tensor_copy(
                            out=qswp[a:a + 32, :], in_=qkF[b2:b2 + 32, :]),
                            waits=[e_ev, e_tab])
                    sc.add("vector", lambda h: h.tensor_tensor(
                        out=qswp[:], in0=qswp[:], in1=srep[:], op=ALU.mult),
                        waits=[e_s, e_tab])
                    sc.add("vector", lambda h: h.tensor_tensor(
                        out=qkF[:], in0=qkF[:], in1=crep[:], op=ALU.mult),
                        waits=[e_ev])
                    e_rope = sc.add("vector", lambda h, mc=mc, dstl=dstl: h.tensor_tensor(
                        out=dstl[mc][:], in0=qkF[:], in1=qswp[:], op=ALU.add))
            e_qkv_ev = sc.evt("scalar")
            for pr in range(2):
                e_vt = 0
                e_vt_hist = {}
                for tt in range(NTC):
                    ps = psS_[tt % 2]
                    wt = [e_wl, e_qkv_ev,
                          e_rope if pr == 0 else e_on]
                    if pr == 0:
                        wt.append(e_von2)
                    if tt >= 2:
                        wt.append(e_vt_hist[tt - 2])
                    e_mm = 0
                    for k in range(NDC):
                        e_mm = sc.add("tensor", lambda h, k=k, tt=tt, pr=pr, ps=ps: h.matmul(
                            out=ps[:, 0:128],
                            lhsT=xhF[k][:, tt * 128:(tt + 1) * 128],
                            rhs=wkv[k][:, 512 + pr * 128:512 + (pr + 1) * 128],
                            start=(k == 0), stop=(k == NDC - 1)),
                            waits=wt if k == 0 else [])
                    sc.add("scalar", lambda h, tt=tt, ps=ps: h.copy(
                        out=vtok[:, tt * 132:tt * 132 + 64], in_=ps[:, 0:64]),
                        waits=[e_mm])
                    e_vt = sc.add("scalar", lambda h, tt=tt, ps=ps: h.copy(
                        out=vtok[:, tt * 132 + 66:tt * 132 + 130], in_=ps[:, 64:128]),
                        waits=[])
                    e_vt_hist[tt] = e_vt
                for qh in range(NQH):
                    e_exp = {}
                    e_pv = 0
                    e_pv_prev = 0
                    for kc in range(NKC):
                        wt = [e_vt, e_rope]
                        if kc > 0:
                            wt.append(e_exp[(kc - 1, 1)])
                        nqg = QHW // 512
                        if kc == 0 and (qh > 0 or pr > 0 or hg > 0):
                            wt.append(e_on)
                        e_smm = 0
                        for xy in range(2):
                            for qg in range(nqg):
                                e_smm = sc.add("tensor", lambda h, pr=pr, qh=qh, kc=kc, xy=xy, qg=qg: h.matmul(
                                    out=psS_[xy][:, qg * 512:(qg + 1) * 512],
                                    lhsT=khB[pr][xy * 64:(xy + 1) * 64,
                                                 kc * 128:(kc + 1) * 128],
                                    rhs=qhB[pr][xy * 64:(xy + 1) * 64,
                                                qh * QHW + qg * 512:qh * QHW + (qg + 1) * 512],
                                    start=True, stop=True,
                                    tile_position=(xy * 64, 0)),
                                    waits=wt if (xy == 0 and qg == 0) else [])
                        for xy in range(2):
                            wte = [e_smm]
                            if e_pv_prev:
                                wte.append(e_pv_prev)
                            e_exp[(kc, xy)] = sc.add("scalar", lambda h, xy=xy: h.activation(
                                out=expb[xy][:], in_=psS_[xy][:],
                                func=AF.Exp, scale=0.125, bias=C("neg15")),
                                waits=wte)
                        for xy in range(2):
                            for qg in range(nqg):
                                e_pv = sc.add("tensor", lambda h, xy=xy, qg=qg, kc=kc: h.matmul(
                                    out=psO[xy][0:65, qg * 512:(qg + 1) * 512],
                                    lhsT=vtok[:, kc * 132 + xy * 66:kc * 132 + xy * 66 + 65],
                                    rhs=expb[xy][:, qg * 512:(qg + 1) * 512],
                                    start=(kc == 0), stop=(kc == NKC - 1)),
                                    waits=[e_exp[(kc, xy)]] if qg == 0 else [])
                        e_pv_prev = e_pv
                    for xy in range(2):
                        head = 4 * hg + 2 * pr + xy
                        e_dr = sc.add("vector", lambda h, xy=xy: h.tensor_copy(
                            out=drow[:], in_=psO[xy][64:65, :]),
                            waits=[e_pv])
                        e_rcp = sc.add("vector", lambda h: h.reciprocal(
                            out=drow[:], in_=drow[:]))
                        e_dm = 0
                        for qg in range(QHW // 512):
                            wt5 = [e_rcp, e_ones, e_exp[(NKC - 1, xy)]]
                            if e_on:
                                wt5.append(e_on)
                            e_dm = sc.add("tensor", lambda h, xy=xy, qg=qg: h.matmul(
                                out=psS_[xy][0:64, qg * 512:(qg + 1) * 512],
                                lhsT=onesrow[0:1, 0:64],
                                rhs=drow[0:1, qg * 512:(qg + 1) * 512],
                                start=True, stop=True),
                                waits=wt5 if qg == 0 else [])
                        e_db = sc.add("scalar", lambda h, xy=xy: h.copy(
                            out=drep[:], in_=psS_[xy][0:64, :]), waits=[e_dm])
                        e_on = sc.add("vector", lambda h, xy=xy, head=head, qh=qh: h.tensor_tensor(
                            out=obuf[head // 2][(head % 2) * 64:(head % 2) * 64 + 64,
                                                qh * QHW:(qh + 1) * QHW],
                            in0=psO[xy][0:64, :], in1=drep[:], op=ALU.mult),
                            waits=[e_db])
        Bo = sc.barrier()
        ar.reset(mk2)
        psX = psS_ if QHW == D else [psum(stk, [128, D], F32) for _ in range(2)]
        wob = [ar.alloc([128, D], BF16) for _ in range(NDC)]
        e_wol = 0
        for k in range(NDC):
            e_wol = sc.add("sync", lambda h, k=k: h.dma_start(
                out=wob[k][:], in_=wod[k * 128:(k + 1) * 128, :]),
                waits=(Bo + [e_wo]) if k == 0 else (), dma=True)
        e_resh = []
        for i in range(NTC):
            ps = psX[i % 2]
            wt = [e_wol, e_on]
            if len(e_resh) >= 2:
                wt.append(e_resh[-2])
            e_mm = 0
            for k in range(NDC):
                for eh in range(2):
                    e_mm = sc.add("tensor", lambda h, i=i, k=k, eh=eh, ps=ps: h.matmul(
                        out=ps[:, eh * 512:(eh + 1) * 512],
                        lhsT=obuf[k][:, i * 128:(i + 1) * 128],
                        rhs=wob[k][:, eh * 512:(eh + 1) * 512],
                        start=(k == 0), stop=(k == NDC - 1)),
                        waits=wt if (k == 0 and eh == 0) else [])
            e_resh.append(sc.add("vector", lambda h, i=i, ps=ps: h.tensor_tensor(
                out=xt[:, i * D:(i + 1) * D], in0=ps[:, :],
                in1=xt[:, i * D:(i + 1) * D], op=ALU.add),
                waits=[e_mm]))
        psclose()
        ar.reset(mkA)
        sc.barrier()
        return sc.evt("vector")

    e_store = 0
    e_xrdy = e_xload
    for b in range(2):
        ffn(e_xrdy)
        att(e_xrdy)
        e_done = ffn(e_xrdy)
        for i in range(NTC):
            e_store = sc.add("gpsimd", lambda h, b=b, i=i: h.dma_start(
                out=x_out[b, i * 128:(i + 1) * 128, :],
                in_=xt[:, i * D:(i + 1) * D]),
                waits=[e_done] if i == 0 else (), dma=True)
        if b == 0:
            e_xrdy = load_x(1, waits=[e_store])

    sc.add("sync", "SEMINC", waits=[e_store])

    with ExitStack() as stk:
        sems = {k: stk.enter_context(nc.semaphore(f"sem_{k}")) for k in Sched.KEYS}
        block = stk.enter_context(nc.Block())
        sc.emit(block, sems)
    return nc


_NC_CACHE = [None]


def kernel(L0_attn_x, L0_attn_w_qkv, L0_attn_w_o, L0_ffn_w1, L0_ffn_conv_w,
           L0_ffn_conv_b, L0_ffn_w2):
    x = np.ascontiguousarray(L0_attn_x, dtype=np.float32)
    wqkv = np.ascontiguousarray(L0_attn_w_qkv, dtype=np.float32)
    wo = np.ascontiguousarray(L0_attn_w_o, dtype=np.float32)
    w1t = np.ascontiguousarray(np.asarray(L0_ffn_w1).T, dtype=np.float32)
    w2t = np.ascontiguousarray(np.asarray(L0_ffn_w2).T, dtype=np.float32)
    cwv = np.ascontiguousarray(
        np.asarray(L0_ffn_conv_w).reshape(NHC, 128, 3).transpose(1, 0, 2)
        .reshape(128, NHC * 3), dtype=np.float32)
    cbv = np.ascontiguousarray(
        np.asarray(L0_ffn_conv_b).reshape(NHC, 128).T, dtype=np.float32)
    ident = np.eye(128, dtype=ml_dtypes.bfloat16)
    ident16 = np.eye(16, dtype=np.float32)
    consts = np.zeros((128, 8), np.float32)
    consts[:, CN["half"]] = 0.5
    consts[:, CN["neghalf"]] = -0.5
    consts[:, CN["neg15"]] = -15.0
    consts[:, CN["one"]] = 1.0

    if _NC_CACHE[0] is None:
        _NC_CACHE[0] = build_program()
    nc = _NC_CACHE[0]

    in_maps = []
    for c in range(8):
        in_maps.append({
            "x_in": x[2 * c:2 * c + 2],
            "w_qkv": wqkv, "w_o": wo, "w1t": w1t, "w2t": w2t,
            "cwt": cwv, "cbt": cbv, "identin": ident, "ident16in": ident16,
            "constin": consts,
        })
    res = run_bass_kernel_spmd(nc, in_maps, core_ids=list(range(8)))
    out = np.concatenate([r["x_out"] for r in res.results], axis=0)
    return np.ascontiguousarray(out, dtype=np.float32)


# version 09
# speedup vs baseline: 1.1658x; 1.1658x over previous
import math
import numpy as np
import ml_dtypes
from contextlib import ExitStack

import concourse.bass as bass
import concourse.mybir as mybir
from concourse.bass_utils import run_bass_kernel_spmd

F32 = mybir.dt.float32
BF16 = mybir.dt.bfloat16
I32 = mybir.dt.int32
AF = mybir.ActivationFunctionType
ALU = mybir.AluOpType
AX = mybir.AxisListType

T = 2048
D = 1024
HID = 4096
NTC = T // 128
NDC = D // 128
NHC = HID // 128
GA = 256
NGA = T // GA
GB = 512
NGB = T // GB
NHG = 4
NKC = T // 128
QHW = 1024 if T >= 1024 else T
NQH = T // QHW

SBUF_BASE = 18432
SBUF_CAP = 229376

_TWO_PI = 2.0 * math.pi
_C1 = math.floor(_TWO_PI * 512.0) / 512.0
_C2 = float(np.float32(math.floor((_TWO_PI - _C1) * (1 << 22)) / (1 << 22)))
_C3 = _TWO_PI - _C1 - _C2
_PI_F32 = float(np.float32(math.pi))
_PIH_HI = float(np.float32(math.pi / 2))
_PIH_LO = math.pi / 2 - _PIH_HI
_LN_HI = float(np.float32(math.log(10000.0)))
_LN_LO = math.log(10000.0) - _LN_HI

CN = {"half": 0, "neghalf": 1, "neg15": 2, "one": 3}


def _set_T(newT):
    global T, NTC, NGA, NGB, NKC, QHW, NQH
    T = newT
    NTC = T // 128
    NGA = T // GA
    NGB = T // GB
    NKC = T // 128
    QHW = 1024 if T >= 1024 else T
    NQH = T // QHW


class Sched:
    ENGINES = ("tensor", "vector", "scalar", "gpsimd", "sync")
    DKEY = {"sync": "syncd", "gpsimd": "gpsd", "vector": "vecd",
            "scalar": "scad", "tensor": "tend"}
    KEYS = ENGINES + ("syncd", "gpsd", "vecd", "scad", "tend")

    def __init__(self):
        self.steps = {e: [] for e in self.ENGINES}
        self.cnt = {k: 0 for k in self.KEYS}

    def add(self, eng, fn, waits=(), inc=True, dma=False):
        w = []
        for it in waits:
            if not it:
                continue
            k, v = it
            if k != eng and v > 0:
                w.append((k, v))
        self.steps[eng].append((fn, w, inc, dma))
        key = self.DKEY[eng] if dma else eng
        if inc:
            self.cnt[key] += 16 if dma else 1
        return (key, self.cnt[key])

    def evt(self, eng):
        return (eng, self.cnt[eng])

    def snap(self):
        return [(k, self.cnt[k]) for k in self.KEYS]

    def barrier(self):
        s = self.snap()
        for e in self.ENGINES:
            self.add(e, "SEMINC", waits=s)
        return self.snap()

    def emit(self, block, sems):
        for eng in self.ENGINES:
            steps = self.steps[eng]
            if not steps:
                continue

            def body(h, steps=steps, eng=eng):
                hi = {k: 0 for k in self.KEYS}
                dkey = self.DKEY[eng]
                done = 0
                for fn, waits, inc, dma in steps:
                    if eng == "vector" and done > hi[eng]:
                        h.wait_ge(sems[eng], done)
                        hi[eng] = done
                    for (wk, wv) in waits:
                        if wv > hi[wk]:
                            h.wait_ge(sems[wk], wv)
                            hi[wk] = wv
                    if fn == "SEMINC":
                        h.sem_inc(sems[eng], 1)
                        done += 1
                        continue
                    ins = fn(h)
                    if inc:
                        ins.then_inc(sems[dkey if dma else eng], 16 if dma else 1)
                        if not dma:
                            done += 1

            getattr(block, eng)(body)


class Arena:
    def __init__(self, nc):
        self.nc = nc
        self.off = SBUF_BASE
        self.n = 0
        self.peak = 0

    def alloc(self, shape, dtype, align=64, at=None):
        nbytes = int(np.prod(shape[1:])) * mybir.dt.size(dtype)
        if at is None:
            self.off = (self.off + align - 1) // align * align
            off = self.off
            self.off += nbytes
            self.peak = max(self.peak, self.off)
            assert self.off <= SBUF_CAP, f"SBUF overflow: {self.off}"
        else:
            off = at
        t = self.nc.alloc_sbuf_tensor_at(
            f"sb{self.n}", list(shape), dtype, offset=off)
        self.n += 1
        self.last = off
        return t

    def mark(self):
        return self.off

    def reset(self, mark):
        self.off = mark


def build_program():
    nc = bass.Bass("TRN2", target_bir_lowering=False, debug=False,
                   detect_race_conditions=False)

    x_in = nc.dram_tensor("x_in", [2, T, D], F32, kind="ExternalInput").ap()
    w_qkv = nc.dram_tensor("w_qkv", [D, 3 * D], F32, kind="ExternalInput").ap()
    w_o = nc.dram_tensor("w_o", [D, D], F32, kind="ExternalInput").ap()
    w1t_d = nc.dram_tensor("w1t", [D, HID], F32, kind="ExternalInput").ap()
    w2t_d = nc.dram_tensor("w2t", [HID, D], F32, kind="ExternalInput").ap()
    cwt = nc.dram_tensor("cwt", [128, 3 * NHC], F32, kind="ExternalInput").ap()
    cbt = nc.dram_tensor("cbt", [128, NHC], F32, kind="ExternalInput").ap()
    identin = nc.dram_tensor("identin", [128, 128], BF16, kind="ExternalInput").ap()
    ident16in = nc.dram_tensor("ident16in", [16, 16], F32, kind="ExternalInput").ap()
    constin = nc.dram_tensor("constin", [128, 8], F32, kind="ExternalInput").ap()
    x_out = nc.dram_tensor("x_out", [2, T, D], F32, kind="ExternalOutput").ap()

    wq1d = nc.dram_tensor("wq1d", [D, HID], BF16, kind="Internal").ap()
    wq2d = nc.dram_tensor("wq2d", [HID, D], BF16, kind="Internal").ap()
    wqkvd = nc.dram_tensor("wqkvd", [D, 3 * D], BF16, kind="Internal").ap()
    wod = nc.dram_tensor("wod", [D, D], BF16, kind="Internal").ap()
    hqd = nc.dram_tensor("hqd", [NHC, 128, T], BF16, kind="Internal").ap()
    statd = nc.dram_tensor("statd", [2 * T], F32, kind="Internal").ap()
    roped = nc.dram_tensor("roped", [2, 32, T], F32, kind="Internal").ap()

    sc = Sched()
    ar = Arena(nc)
    psum_stk = []

    def psopen():
        psum_stk.append(ExitStack())
        return psum_stk[-1]

    def psclose():
        psum_stk.pop().close()

    uid = [0]

    def psum(stk, shape, dtype=F32):
        uid[0] += 1
        return stk.enter_context(nc.psum_tensor(f"ps{uid[0]}", list(shape), dtype))

    xt = ar.alloc([128, NTC * D], F32)
    consts = ar.alloc([128, 8], F32)
    ident = ar.alloc([128, 128], BF16)
    ident16 = ar.alloc([16, 16], F32)
    cw = ar.alloc([128, 3 * NHC], F32)
    cb = ar.alloc([128, NHC], F32)
    smallv = ar.alloc([128, 4], F32)
    onesrow = ar.alloc([1, 128], F32)
    RESIDENT = ar.mark()

    def C(name):
        return consts[:, CN[name]:CN[name] + 1]

    def fp32_transpose(srcap, np_, nf, dstap, scr_f32, scr_bf, psT3, waits):
        ev_ad = None
        for part in range(3):
            if part == 0:
                e_c = sc.add("vector", lambda h: h.tensor_copy(
                    out=scr_bf, in_=srcap), waits=waits)
                e_d = sc.add("vector", lambda h: h.tensor_tensor(
                    out=scr_f32, in0=srcap, in1=scr_bf, op=ALU.subtract))
            else:
                e_c = sc.add("vector", lambda h: h.tensor_copy(
                    out=scr_bf, in_=scr_f32), waits=[ev_ad] if ev_ad else [])
                e_d = sc.add("vector", lambda h: h.tensor_tensor(
                    out=scr_f32, in0=scr_f32, in1=scr_bf, op=ALU.subtract))
            e_t = sc.add("tensor", lambda h: h.transpose(
                out=psT3[0:nf, 0:np_], in_=scr_bf, identity=ident[:]),
                waits=[e_c, e_ident])
            if part == 0:
                ev_ad = sc.add("scalar", lambda h: h.copy(
                    out=dstap, in_=psT3[0:nf, 0:np_]), waits=[e_t])
            else:
                ev_ad = sc.add("vector", lambda h: h.tensor_tensor(
                    out=dstap, in0=dstap, in1=psT3[0:nf, 0:np_], op=ALU.add),
                    waits=[e_t, ev_ad])
        return ev_ad

    e_const = sc.add("sync", lambda h: h.dma_start(out=consts[:], in_=constin[:]), dma=True)
    e_ident = sc.add("sync", lambda h: h.dma_start(out=ident[:], in_=identin[:]), dma=True)
    e_id16 = sc.add("sync", lambda h: h.dma_start(out=ident16[:], in_=ident16in[:]), dma=True)
    e_cw = sc.add("sync", lambda h: h.dma_start(out=cw[:], in_=cwt[:]), dma=True)
    e_cb = sc.add("sync", lambda h: h.dma_start(out=cb[:], in_=cbt[:]), dma=True)
    e_ones = sc.add("vector", lambda h: h.memset(onesrow[:], 1.0))

    def load_x(b, waits=()):
        ev = None
        for i in range(NTC):
            ev = sc.add("gpsimd", lambda h, i=i, b=b: h.dma_start(
                out=xt[:, i * D:(i + 1) * D],
                in_=x_in[b, i * 128:(i + 1) * 128, :]),
                waits=waits if i == 0 else (), dma=True)
        return ev

    e_xload = load_x(0)

    m0 = ar.mark()
    ps0 = psopen()
    tvec = ar.alloc([1, T], F32)
    iv1 = ar.alloc([1, 32], F32)
    iv2 = ar.alloc([1, 32], F32)
    ftab = ar.alloc([32, T], F32)
    rtab = ar.alloc([32, T], F32)
    ntab = ar.alloc([32, T], F32)
    gtab = ar.alloc([32, T], F32)
    ntabi = ar.alloc([32, T], I32)
    sinM = ar.alloc([32, T], F32)
    cosM = ar.alloc([32, T], F32)
    psF = psum(ps0, [32, 512], F32)

    e_tv = sc.add("gpsimd", lambda h: h.iota(
        tvec[:], pattern=[[1, T]], base=0, channel_multiplier=0,
        allow_small_or_imprecise_dtypes=True))
    e_i1 = sc.add("gpsimd", lambda h: h.iota(
        iv1[:], pattern=[[1, 32]], base=0, channel_multiplier=0,
        allow_small_or_imprecise_dtypes=True))
    e_i2 = sc.add("gpsimd", lambda h: h.iota(
        iv2[:], pattern=[[1, 32]], base=0, channel_multiplier=0,
        allow_small_or_imprecise_dtypes=True))
    sc.add("vector", lambda h: h.tensor_scalar(
        out=iv1[:], in0=iv1[:], scalar1=-_LN_HI / 32.0, scalar2=None, op0=ALU.mult),
        waits=[e_i1])
    e_v2 = sc.add("vector", lambda h: h.scalar_tensor_tensor(
        out=iv1[:], in0=iv2[:], scalar=-_LN_LO / 32.0, op0=ALU.mult, op1=ALU.add,
        in1=iv1[:]), waits=[e_i2])
    e_exp = sc.add("scalar", lambda h: h.activation(
        out=iv1[:], in_=iv1[:], func=AF.Exp), waits=[e_v2])
    e_cp = 0
    for tg in range(T // 512):
        wt = [e_exp, e_tv]
        if tg:
            wt.append(e_cp)
        e_mmf = sc.add("tensor", lambda h, tg=tg: h.matmul(
            out=psF[:, :], lhsT=iv1[:], rhs=tvec[:, tg * 512:(tg + 1) * 512],
            start=True, stop=True), waits=wt)
        e_cp = sc.add("vector", lambda h, tg=tg: h.tensor_copy(
            out=ftab[:, tg * 512:(tg + 1) * 512], in_=psF[:, :]),
            waits=[e_mmf])
    sc.add("vector", lambda h: h.tensor_scalar(
        out=ntab[:], in0=ftab[:], scalar1=1.0 / _TWO_PI, scalar2=None, op0=ALU.mult))
    sc.add("vector", lambda h: h.tensor_scalar(
        out=ntabi[:], in0=ntab[:], scalar1=0.0, scalar2=None, op0=ALU.add))
    sc.add("vector", lambda h: h.tensor_copy(out=gtab[:], in_=ntabi[:]))
    sc.add("vector", lambda h: h.tensor_tensor(
        out=ntab[:], in0=gtab[:], in1=ntab[:], op=ALU.is_gt))
    sc.add("vector", lambda h: h.tensor_tensor(
        out=ntab[:], in0=gtab[:], in1=ntab[:], op=ALU.subtract))
    for c in (_C1, _C2, _C3):
        sc.add("vector", lambda h, c=c: h.scalar_tensor_tensor(
            out=rtab[:] if c != _C1 else rtab[:], in0=ntab[:], scalar=-c,
            op0=ALU.mult, op1=ALU.add, in1=ftab[:] if c == _C1 else rtab[:]))
    sc.add("vector", lambda h: h.tensor_scalar(
        out=gtab[:], in0=rtab[:], scalar1=_PI_F32, scalar2=None, op0=ALU.is_gt))
    for c in (_C1, _C2, _C3):
        sc.add("vector", lambda h, c=c: h.scalar_tensor_tensor(
            out=rtab[:], in0=gtab[:], scalar=-c, op0=ALU.mult, op1=ALU.add,
            in1=rtab[:]))
    e_cl = sc.add("vector", lambda h: h.tensor_scalar(
        out=rtab[:], in0=rtab[:], scalar1=_PI_F32, scalar2=-_PI_F32,
        op0=ALU.min, op1=ALU.max))
    e_sin = sc.add("scalar", lambda h: h.activation(
        out=sinM[:], in_=rtab[:], func=AF.Sin), waits=[e_cl])
    e_abs = sc.add("scalar", lambda h: h.activation(
        out=rtab[:], in_=rtab[:], func=AF.Abs))
    sc.add("vector", lambda h: h.tensor_scalar(
        out=rtab[:], in0=rtab[:], scalar1=-1.0, scalar2=_PIH_HI,
        op0=ALU.mult, op1=ALU.add), waits=[e_abs])
    e_r2 = sc.add("vector", lambda h: h.tensor_scalar(
        out=rtab[:], in0=rtab[:], scalar1=_PIH_LO, scalar2=None, op0=ALU.add))
    e_cos = sc.add("scalar", lambda h: h.activation(
        out=cosM[:], in_=rtab[:], func=AF.Sin), waits=[e_r2])
    e_ropew1 = sc.add("gpsimd", lambda h: h.dma_start(
        out=roped[0], in_=sinM[:]), waits=[e_sin], dma=True)
    e_ropew2 = sc.add("gpsimd", lambda h: h.dma_start(
        out=roped[1], in_=cosM[:]), waits=[e_cos], dma=True)
    psclose()

    ps0 = psopen()
    wbuf = [ar.alloc([128, HID], F32) for _ in range(2)]
    wabs = ar.alloc([128, HID], F32)
    wsgn = ar.alloc([128, HID], BF16)
    wqb = ar.alloc([128, HID], BF16)
    wsum = ar.alloc([128, 8], F32)
    wtot = ar.alloc([128, 1], F32)
    w11 = ar.alloc([1, 2], F32)
    psW = psum(ps0, [1, 2], F32)
    psW2 = psum(ps0, [128, 2], F32)

    def wquant(load_chunk, store_chunk, nch, ncol, nelem, sw_col, invsw_col):
        ev_red = []
        e_l = 0
        for c in range(nch):
            buf = wbuf[c % 2]
            wprev = [ev_red[c - 2]] if c >= 2 else []
            e_l = load_chunk(c, buf, wprev)
            ev_red.append(sc.add("vector", lambda h, c=c, buf=buf: h.tensor_reduce(
                out=wsum[:, c:c + 1], in_=buf[:, 0:ncol], axis=AX.X,
                op=ALU.add, apply_absolute_value=True), waits=[e_l]))
        e_t = sc.add("vector", lambda h: h.tensor_reduce(
            out=wtot[:], in_=wsum[:, 0:nch], axis=AX.X, op=ALU.add))
        e_mm = sc.add("tensor", lambda h: h.matmul(
            out=psW[0:1, 0:1], lhsT=wtot[:], rhs=C("one"), start=True, stop=True),
            waits=[e_t, e_const])
        sc.add("vector", lambda h: h.tensor_copy(out=w11[0:1, 0:1], in_=psW[0:1, 0:1]),
               waits=[e_mm])
        sc.add("vector", lambda h: h.tensor_scalar(
            out=w11[0:1, 0:1], in0=w11[0:1, 0:1], scalar1=1.0 / nelem,
            scalar2=1e-5, op0=ALU.mult, op1=ALU.max))
        e_c3 = sc.add("vector", lambda h: h.reciprocal(
            out=w11[0:1, 1:2], in_=w11[0:1, 0:1]))
        e_bm = sc.add("tensor", lambda h: h.matmul(
            out=psW2[:, :], lhsT=onesrow[:], rhs=w11[:], start=True, stop=True),
            waits=[e_c3, e_ones])
        e_bc = sc.add("vector", lambda h, sw_col=sw_col, invsw_col=invsw_col: h.tensor_copy(
            out=smallv[:, invsw_col:invsw_col + 1], in_=psW2[:, 0:1]),
            waits=[e_bm])
        e_bc = sc.add("vector", lambda h, sw_col=sw_col: h.tensor_copy(
            out=smallv[:, sw_col:sw_col + 1], in_=psW2[:, 1:2]))
        ev_q = []
        e_st = 0
        for c in range(nch):
            buf = wbuf[c % 2]
            wprev = [ev_q[c - 2]] if c >= 2 else [ev_red[-1]]
            e_l = load_chunk(c, buf, wprev)
            e_a = sc.add("scalar", lambda h, buf=buf: h.activation(
                out=wabs[:, 0:ncol], in_=buf[:, 0:ncol], func=AF.Abs,
                scale=smallv[:, sw_col:sw_col + 1]),
                waits=[e_l, e_bc])
            e_s = sc.add("scalar", lambda h, buf=buf: h.activation(
                out=wsgn[:, 0:ncol], in_=buf[:, 0:ncol], func=AF.Sign))
            sc.add("vector", lambda h: h.tensor_scalar(
                out=wabs[:, 0:ncol], in0=wabs[:, 0:ncol], scalar1=0.5,
                scalar2=None, op0=ALU.is_ge), waits=[e_a])
            e_q = sc.add("vector", lambda h: h.tensor_tensor(
                out=wqb[:, 0:ncol], in0=wabs[:, 0:ncol], in1=wsgn[:, 0:ncol],
                op=ALU.mult), waits=[e_s])
            ev_q.append(e_q)
            e_st = store_chunk(c, e_q)
        return e_st

    def w1_load(c, buf, wprev):
        return sc.add("sync", lambda h, c=c, buf=buf: h.dma_start(
            out=buf[:, 0:HID], in_=w1t_d[c * 128:(c + 1) * 128, :]),
            waits=wprev, dma=True)

    def w1_store(c, e_q):
        return sc.add("sync", lambda h, c=c: h.dma_start(
            out=wq1d[c * 128:(c + 1) * 128, :], in_=wqb[:, 0:HID]),
            waits=[e_q], dma=True)

    e_wq1 = wquant(w1_load, w1_store, NDC, HID, D * HID, sw_col=0, invsw_col=1)

    def w2_load(c, buf, wprev):
        ev = None
        for j in range(4):
            ev = sc.add("sync", lambda h, c=c, j=j, buf=buf: h.dma_start(
                out=buf[:, j * D:(j + 1) * D],
                in_=w2t_d[c * 512 + j * 128:c * 512 + (j + 1) * 128, :]),
                waits=wprev if j == 0 else (), dma=True)
        return ev

    def w2_store(c, e_q):
        ev = None
        for j in range(4):
            ev = sc.add("sync", lambda h, c=c, j=j: h.dma_start(
                out=wq2d[c * 512 + j * 128:c * 512 + (j + 1) * 128, :],
                in_=wqb[:, j * D:(j + 1) * D]),
                waits=[e_q] if j == 0 else (), dma=True)
        return ev

    e_wq2 = wquant(w2_load, w2_store, 8, HID, D * HID, sw_col=2, invsw_col=3)

    def wcast(src, dst, nch, ncol):
        ev = None
        evq_h = []
        for c in range(nch):
            buf = wbuf[c % 2]
            wprev = [evq_h[c - 2]] if c >= 2 else []
            e_l = sc.add("sync", lambda h, c=c, buf=buf: h.dma_start(
                out=buf[:, 0:ncol], in_=src[c * 128:(c + 1) * 128, :]),
                waits=wprev, dma=True)
            e_q = sc.add("vector", lambda h, buf=buf: h.tensor_copy(
                out=wqb[:, 0:ncol], in_=buf[:, 0:ncol]), waits=[e_l])
            evq_h.append(e_q)
            ev = sc.add("sync", lambda h, c=c: h.dma_start(
                out=dst[c * 128:(c + 1) * 128, :], in_=wqb[:, 0:ncol]),
                waits=[e_q], dma=True)
        return ev

    e_wqkv = wcast(w_qkv, wqkvd, NDC, 3 * D)
    e_wo = wcast(w_o, wod, NDC, D)
    psclose()
    ar.reset(m0)

    sc.barrier()

    def lnq(want_xq, e_xrdy, outF):
        mk = ar.mark()
        stk = psopen()
        red1 = ar.alloc([128, NTC], F32)
        red2 = ar.alloc([128, NTC], F32)
        rstd = ar.alloc([128, NTC], F32)
        nmr = ar.alloc([128, NTC], F32)
        sxc = ar.alloc([128, NTC], F32)
        invs = ar.alloc([128, NTC], F32)
        sqs = ar.alloc([128, D], F32)
        xhat = [ar.alloc([128, D], F32) for _ in range(2)]
        i32s = [ar.alloc([128, D], I32) for _ in range(2)]
        ihalf = [ar.alloc([128, D], BF16) for _ in range(2)]
        fixb = ar.alloc([128, D], BF16)
        xqT = [ar.alloc([128, D], BF16) for _ in range(2)]
        invsbf = ar.alloc([128, NTC], BF16)
        invsd = ar.alloc([128, NTC], F32)
        statT = ar.alloc([16, 128], F32)
        lnrow = ar.alloc([1, T], F32) if want_xq else None
        psT = [psum(stk, [128, 512], BF16) for _ in range(2)]
        psG = psum(stk, [1, NTC * 128], F32) if want_xq else None

        for i in range(NTC):
            xi = xt[:, i * D:(i + 1) * D]
            sc.add("vector", lambda h, xi=xi, i=i: h.tensor_reduce(
                out=red1[:, i:i + 1], in_=xi, axis=AX.X, op=ALU.add),
                waits=[e_xrdy] if i == 0 else [])
            sc.add("scalar", lambda h, xi=xi, i=i: h.activation(
                out=sqs[:], in_=xi, func=AF.Square, accum_out=red2[:, i:i + 1]),
                waits=[e_xrdy] if i == 0 else [])
        e_sq = sc.evt("scalar")
        sc.add("vector", lambda h: h.tensor_scalar(
            out=red1[:], in0=red1[:], scalar1=1.0 / D, scalar2=None, op0=ALU.mult),
            waits=[e_sq])
        sc.add("vector", lambda h: h.tensor_scalar(
            out=red2[:], in0=red2[:], scalar1=1.0 / D, scalar2=None, op0=ALU.mult))
        sc.add("vector", lambda h: h.tensor_tensor(
            out=rstd[:], in0=red1[:], in1=red1[:], op=ALU.mult))
        sc.add("vector", lambda h: h.tensor_tensor(
            out=red2[:], in0=red2[:], in1=rstd[:], op=ALU.subtract))
        e_ve = sc.add("vector", lambda h: h.tensor_scalar(
            out=red2[:], in0=red2[:], scalar1=1e-5, scalar2=None, op0=ALU.add))
        e_sd = sc.add("scalar", lambda h: h.activation(
            out=rstd[:], in_=red2[:], func=AF.Sqrt), waits=[e_ve])
        sc.add("vector", lambda h: h.reciprocal(out=rstd[:], in_=rstd[:]),
               waits=[e_sd])
        sc.add("vector", lambda h: h.tensor_tensor(
            out=nmr[:], in0=rstd[:], in1=rstd[:], op=ALU.mult))
        sc.add("vector", lambda h: h.tensor_tensor(
            out=nmr[:], in0=nmr[:], in1=red2[:], op=ALU.mult))
        sc.add("vector", lambda h: h.tensor_scalar(
            out=nmr[:], in0=nmr[:], scalar1=-0.5, scalar2=1.5,
            op0=ALU.mult, op1=ALU.add))
        sc.add("vector", lambda h: h.tensor_tensor(
            out=rstd[:], in0=rstd[:], in1=nmr[:], op=ALU.mult))
        sc.add("vector", lambda h: h.tensor_tensor(
            out=nmr[:], in0=red1[:], in1=rstd[:], op=ALU.mult))
        e_b = sc.add("vector", lambda h: h.tensor_scalar(
            out=nmr[:], in0=nmr[:], scalar1=-1.0, scalar2=None, op0=ALU.mult))

        e_tr = []
        e_q_hist = {}
        e_i32_hist = {}
        e_trT_hist = {}
        for i in range(NTC):
            xi = xt[:, i * D:(i + 1) * D]
            xh = xhat[i % 2]
            i32 = i32s[i % 2]
            ih = ihalf[i % 2]
            xq = xqT[i % 2]
            if want_xq:
                wt = [e_b] if i == 0 else []
                if i >= 2:
                    wt.append(e_i32_hist[i - 2])
                sc.add("vector", lambda h, xi=xi, xh=xh, i=i: h.tensor_scalar(
                    out=xh[:], in0=xi, scalar1=rstd[:, i:i + 1],
                    scalar2=nmr[:, i:i + 1], op0=ALU.mult, op1=ALU.add), waits=wt)
                sc.add("vector", lambda h, xh=xh, i=i: h.tensor_reduce(
                    out=sxc[:, i:i + 1], in_=xh[:], axis=AX.X, op=ALU.max,
                    apply_absolute_value=True))
                sc.add("vector", lambda h, i=i: h.tensor_scalar(
                    out=sxc[:, i:i + 1], in0=sxc[:, i:i + 1], scalar1=1e-5,
                    scalar2=None, op0=ALU.max))
                sc.add("vector", lambda h, i=i: h.tensor_scalar(
                    out=invs[:, i:i + 1], in0=sxc[:, i:i + 1],
                    scalar1=smallv[:, 1:2], scalar2=1.0 / 127.0,
                    op0=ALU.mult, op1=ALU.mult))
                sc.add("vector", lambda h, i=i: h.reciprocal(
                    out=sxc[:, i:i + 1], in_=sxc[:, i:i + 1]))
                e_sx = sc.add("vector", lambda h, i=i: h.tensor_scalar(
                    out=sxc[:, i:i + 1], in0=sxc[:, i:i + 1], scalar1=127.0,
                    scalar2=None, op0=ALU.mult))
                wt2 = [e_sx]
                if i >= 2:
                    wt2.append(e_q_hist[i - 2])
                e_i32_hist[i] = sc.add("scalar", lambda h, xh=xh, i32=i32, i=i: h.activation(
                    out=i32[:], in_=xh[:], func=AF.Identity,
                    scale=sxc[:, i:i + 1], bias=C("half")), waits=wt2)
                e_ih = sc.add("scalar", lambda h, i32=i32, ih=ih: h.activation(
                    out=ih[:], in_=i32[:], func=AF.Identity, bias=C("neghalf")))
                sc.add("vector", lambda h, xh=xh, ih=ih, i=i: h.scalar_tensor_tensor(
                    out=fixb[:], in0=xh[:], scalar=sxc[:, i:i + 1], op0=ALU.mult,
                    op1=ALU.is_lt, in1=ih[:]), waits=[e_ih])
                wtq = []
                if i >= 2:
                    wtq.append(e_trT_hist[i - 2])
                e_src = sc.add("vector", lambda h, ih=ih, xq=xq: h.scalar_tensor_tensor(
                    out=xq[:], in0=ih[:], scalar=0.5, op0=ALU.add,
                    op1=ALU.subtract, in1=fixb[:]), waits=wtq)
                e_q_hist[i] = e_src
            else:
                wt = [e_b] if i == 0 else []
                if i >= 2:
                    wt.append(e_trT_hist[i - 2])
                e_src = sc.add("vector", lambda h, xi=xi, xq=xq, i=i: h.tensor_scalar(
                    out=xq[:], in0=xi, scalar1=rstd[:, i:i + 1],
                    scalar2=nmr[:, i:i + 1], op0=ALU.mult, op1=ALU.add), waits=wt)
                e_q_hist[i] = e_src
            for k in range(NDC):
                n = i * NDC + k
                ps = psT[n % 2]
                wt3 = [e_src, e_ident]
                if n >= 2:
                    wt3.append(e_tr[n - 2])
                e_t = sc.add("tensor", lambda h, xq=xq, k=k, ps=ps: h.transpose(
                    out=ps[:, 0:128], in_=xq[:, k * 128:(k + 1) * 128],
                    identity=ident[:]), waits=wt3)
                e_trT_hist[i] = e_t
                e_tr.append(sc.add("scalar", lambda h, i=i, k=k, ps=ps: h.copy(
                    out=outF[k][:, i * 128:(i + 1) * 128], in_=ps[:, 0:128]),
                    waits=[e_t]))
        e_s_done = sc.evt("scalar")
        e_statw = None
        if want_xq:
            e_tT = fp32_transpose(invs[:], 128, NTC, statT[0:NTC, :],
                                  invsd[:], invsbf[:], psT[0],
                                  [sc.evt("vector"), e_tr[-1]])
            e_gm = 0
            for i in range(NTC):
                e_gm = sc.add("tensor", lambda h, i=i: h.matmul(
                    out=psG[0:1, i * 128:(i + 1) * 128],
                    lhsT=ident16[0:NTC, i:i + 1], rhs=statT[0:NTC, :],
                    start=True, stop=True),
                    waits=[e_tT, e_id16] if i == 0 else [])
            e_ev = sc.add("scalar", lambda h: h.copy(
                out=lnrow[:], in_=psG[:]), waits=[e_gm])
            e_statw = sc.add("gpsimd", lambda h: h.dma_start(
                out=statd.rearrange("(a t) -> a t", a=1)[:, 0:T], in_=lnrow[:]),
                waits=[e_ev], dma=True)
        sc.barrier()
        psclose()
        ar.reset(mk)
        return e_s_done, e_statw

    def ffn(e_xrdy):
        mkF = ar.mark()
        invs2c = ar.alloc([128, NTC], F32)
        mkB = ar.mark()
        xqF = [ar.alloc([128, T], BF16) for _ in range(NDC)]
        e_lnq_s, e_statw = lnq(True, e_xrdy, xqF)

        mkA = ar.mark()
        stkA = psopen()
        s1row = ar.alloc([1, 258], F32)
        s1row_off = ar.last
        wq1sb = [ar.alloc([128, HID], BF16) for _ in range(NDC)]
        srepg = ar.alloc([128, 258], F32)
        hraw = ar.alloc([128, 258], F32)
        cbuf = ar.alloc([128, 256], F32)
        h2 = [ar.alloc([128, 256], F32) for _ in range(NHC)]
        racc = ar.alloc([128, 256], F32)
        raccT = ar.alloc([128, 256], F32)
        raccT_off = ar.last
        sxr = ar.alloc([128, 256], F32)
        splf = ar.alloc([128, 128], F32)
        splf_off = ar.last
        splb = ar.alloc([128, 128], BF16)
        amax2 = ar.alloc([128, 2], F32)
        sx2t = ar.alloc([128, 2], F32)
        statT2 = ar.alloc([2, 128], F32)
        sx2row = ar.alloc([1, 256], F32, at=s1row_off)
        zb = ar.alloc([128, 256], F32, at=raccT_off)
        i32g = ar.alloc([128, 256], I32)
        rmin = ar.alloc([128, 256], F32)
        ihg = ar.alloc([128, 256], BF16)
        fxg = ar.alloc([128, 256], BF16, at=splf_off)
        hqs_strip = [ar.alloc([128, 256], BF16) for _ in range(2)]
        psH = [psum(stkA, [128, 512], F32) for _ in range(2)]
        psS = psum(stkA, [128, 512], F32)
        psT3 = psum(stkA, [128, 256], BF16)

        e_w1l = 0
        for k in range(NDC):
            e_w1l = sc.add("sync", lambda h, k=k: h.dma_start(
                out=wq1sb[k][:], in_=wq1d[k * 128:(k + 1) * 128, :]),
                waits=[e_wq1] if k == 0 else (), dma=True)

        e_spill = []
        e_se_prev = 0
        e_sil_prev = 0
        e_deq = {}
        e_i_prev = 0
        e_hq_prev = 0
        e_fx_prev = 0
        e_zb_guard = []
        e_sxr = 0
        e_zm = 0
        qg_scale = {}
        e_sil_hist = {}
        for g in range(NGA + 1):
            conv = g < NGA
            quant = g > 0
            if conv:
                lo = max(g * GA - 1, 0)
                hi_ = min(g * GA + GA + 1, T)
                w = hi_ - lo
                coff = 1 if g == 0 else 0
                wt = [e_statw]
                if e_se_prev:
                    wt.append(e_se_prev)
                e_r1 = sc.add("sync", lambda h, lo=lo, w=w: h.dma_start(
                    out=s1row[0:1, 0:w],
                    in_=statd.rearrange("(a t) -> a t", a=1)[:, lo:lo + w]),
                    waits=wt, dma=True)
                e_bm = sc.add("tensor", lambda h, w=w: h.matmul(
                    out=psS[:, 0:w], lhsT=onesrow[:], rhs=s1row[0:1, 0:w],
                    start=True, stop=True),
                    waits=[e_r1, e_ones] + ([e_se_prev] if e_se_prev else []))
                wt_se = [e_bm]
                if g > 0:
                    wt_se.append(e_deq[(g - 1, NHC - 1)])
                e_se = sc.add("scalar", lambda h, w=w, coff=coff: h.copy(
                    out=srepg[:, coff:coff + w], in_=psS[:, 0:w]), waits=wt_se)
                e_se_prev = e_se
                if g == 0:
                    sc.add("vector", lambda h: h.memset(hraw[:, 0:1], 0.0))
                if g == NGA - 1:
                    sc.add("vector", lambda h: h.memset(hraw[:, 257:258], 0.0))
            for m in range(NHC):
                if conv:
                    ps = psH[m % 2]
                    wt1 = [e_w1l, e_lnq_s]
                    if (g * NHC + m) >= 2:
                        n = g * NHC + m - 2
                        wt1.append(e_deq[(n // NHC, n % NHC)])
                    e_mm = 0
                    for k in range(NDC):
                        e_mm = sc.add("tensor", lambda h, k=k, m=m, lo=lo, w=w, ps=ps: h.matmul(
                            out=ps[:, 0:w],
                            lhsT=wq1sb[k][:, m * 128:(m + 1) * 128],
                            rhs=xqF[k][:, lo:lo + w],
                            start=(k == 0), stop=(k == NDC - 1)),
                            waits=wt1 if k == 0 else [])
                    e_dq = sc.add("vector", lambda h, w=w, coff=coff, ps=ps: h.tensor_tensor(
                        out=hraw[:, coff:coff + w], in0=ps[:, 0:w],
                        in1=srepg[:, coff:coff + w], op=ALU.mult),
                        waits=[e_mm, e_se])
                    e_deq[(g, m)] = e_dq
                    wt_c1 = [e_dq, e_cb]
                    if e_sil_prev:
                        wt_c1.append(e_sil_prev)
                    e_c1 = sc.add("gpsimd", lambda h, m=m: h.tensor_scalar(
                        out=cbuf[:], in0=hraw[:, 0:256], scalar1=cw[:, 3 * m:3 * m + 1],
                        scalar2=cb[:, m:m + 1], op0=ALU.mult, op1=ALU.add),
                        waits=wt_c1)
                if quant:
                    e_sxr_g = qg_scale[g - 1]
                    wt4 = [e_sxr_g] if m == 0 else []
                    if e_i_prev:
                        wt4.append(e_i_prev)
                    if e_fx_prev:
                        wt4.append(e_fx_prev)
                    e_zm = sc.add("gpsimd", lambda h, m=m: h.tensor_tensor(
                        out=zb[:], in0=h2[m][:], in1=sxr[:], op=ALU.mult), waits=wt4)
                if conv:
                    e_c2 = sc.add("vector", lambda h, m=m: h.scalar_tensor_tensor(
                        out=cbuf[:], in0=hraw[:, 1:257], scalar=cw[:, 3 * m + 1:3 * m + 2],
                        op0=ALU.mult, op1=ALU.add, in1=cbuf[:]),
                        waits=[e_c1])
                    e_c3 = sc.add("vector", lambda h, m=m: h.scalar_tensor_tensor(
                        out=cbuf[:], in0=hraw[:, 2:258], scalar=cw[:, 3 * m + 2:3 * m + 3],
                        op0=ALU.mult, op1=ALU.add, in1=cbuf[:]))
                if quant:
                    e_i = sc.add("scalar", lambda h: h.activation(
                        out=i32g[:], in_=zb[:], func=AF.Identity, bias=C("half")),
                        waits=[e_zm] + ([e_hq_prev] if e_hq_prev else []))
                    e_i_prev = e_i
                    e_h = sc.add("scalar", lambda h: h.activation(
                        out=ihg[:], in_=i32g[:], func=AF.Identity, bias=C("neghalf")))
                if conv:
                    wt_sil = [e_c3]
                    if quant:
                        wt_sil.append(e_zm)
                    e_sil = sc.add("scalar", lambda h, m=m: h.activation(
                        out=h2[m][:], in_=cbuf[:], func=AF.Silu),
                        waits=wt_sil)
                    e_sil_prev = e_sil
                    e_sil_hist[m] = e_sil
                if quant:
                    e_fx_prev = sc.add("vector", lambda h: h.tensor_tensor(
                        out=fxg[:], in0=ihg[:], in1=zb[:], op=ALU.is_gt),
                        waits=[e_h])
                    strip = hqs_strip[m % 2]
                    wt_hq = [e_spill[-2]] if len(e_spill) >= 2 else []
                    e_hq = sc.add("vector", lambda h, strip=strip: h.scalar_tensor_tensor(
                        out=strip[:], in0=ihg[:], scalar=0.5, op0=ALU.add,
                        op1=ALU.subtract, in1=fxg[:]), waits=wt_hq)
                    e_hq_prev = e_hq
                    e_spill.append(sc.add("gpsimd", lambda h, g=g, m=m, strip=strip: h.dma_start(
                        out=hqd[m, :, (g - 1) * GA:g * GA], in_=strip[:]),
                        waits=[e_hq], dma=True))
                if conv and m > 0:
                    mm1 = m - 1
                    wt_r = [e_sil_hist[mm1]]
                    if mm1 == 0:
                        e_r = sc.add("vector", lambda h: h.tensor_copy(
                            out=racc[:], in_=h2[0][:]), waits=wt_r)
                        e_rG = sc.add("vector", lambda h: h.tensor_copy(
                            out=rmin[:], in_=h2[0][:]), waits=wt_r)
                    else:
                        e_r = sc.add("vector", lambda h, mm1=mm1: h.tensor_tensor(
                            out=racc[:], in0=racc[:], in1=h2[mm1][:], op=ALU.max),
                            waits=wt_r)
                        e_rG = sc.add("vector", lambda h, mm1=mm1: h.tensor_tensor(
                            out=rmin[:], in0=rmin[:], in1=h2[mm1][:], op=ALU.min),
                            waits=wt_r)
            if quant:
                e_zb_guard = [e_zm, e_i_prev]
            if not conv:
                break
            mm1 = NHC - 1
            e_r = sc.add("vector", lambda h, mm1=mm1: h.tensor_tensor(
                out=racc[:], in0=racc[:], in1=h2[mm1][:], op=ALU.max),
                waits=[e_sil_hist[mm1]])
            e_rG = sc.add("vector", lambda h, mm1=mm1: h.tensor_tensor(
                out=rmin[:], in0=rmin[:], in1=h2[mm1][:], op=ALU.min),
                waits=[e_sil_hist[mm1]])
            e_tT = 0
            for blk in range(2):
                e_tT = fp32_transpose(racc[:, blk * 128:(blk + 1) * 128], 128, 128,
                                      raccT[:, blk * 128:(blk + 1) * 128],
                                      splf[:], splb[:], psT3,
                                      [e_r] + ([e_tT] if e_tT else [])
                                      + (e_zb_guard if blk == 0 else []))
                sc.add("vector", lambda h, blk=blk: h.tensor_reduce(
                    out=amax2[:, blk:blk + 1],
                    in_=raccT[:, blk * 128:(blk + 1) * 128], axis=AX.X,
                    op=ALU.max), waits=[e_tT])
            for blk in range(2):
                e_tT = fp32_transpose(rmin[:, blk * 128:(blk + 1) * 128], 128, 128,
                                      raccT[:, blk * 128:(blk + 1) * 128],
                                      splf[:], splb[:], psT3,
                                      [e_rG, e_tT])
                sc.add("vector", lambda h, blk=blk: h.tensor_reduce(
                    out=sx2t[:, blk:blk + 1],
                    in_=raccT[:, blk * 128:(blk + 1) * 128], axis=AX.X,
                    op=ALU.min), waits=[e_tT])
            sc.add("vector", lambda h: h.tensor_scalar(
                out=sx2t[:], in0=sx2t[:], scalar1=-1.0, scalar2=None,
                op0=ALU.mult))
            sc.add("vector", lambda h: h.tensor_tensor(
                out=amax2[:], in0=amax2[:], in1=sx2t[:], op=ALU.max))
            sc.add("vector", lambda h: h.tensor_scalar(
                out=amax2[:], in0=amax2[:], scalar1=1e-5, scalar2=None,
                op0=ALU.max))
            sc.add("vector", lambda h, g=g: h.tensor_scalar(
                out=invs2c[:, 2 * g:2 * g + 2], in0=amax2[:],
                scalar1=smallv[:, 3:4], scalar2=0.5 / 127.0,
                op0=ALU.mult, op1=ALU.mult))
            sc.add("vector", lambda h: h.reciprocal(out=sx2t[:], in_=amax2[:]))
            e_sx2t = sc.add("vector", lambda h: h.tensor_scalar(
                out=sx2t[:], in0=sx2t[:], scalar1=127.0, scalar2=None,
                op0=ALU.mult))
            e_tT2 = fp32_transpose(sx2t[:], 128, 2, statT2[:],
                                   splf[:, 0:2], splb[:, 0:2], psT3, [e_sx2t])
            e_g2 = 0
            for j in range(2):
                e_g2 = sc.add("tensor", lambda h, j=j: h.matmul(
                    out=psS[0:1, j * 128:(j + 1) * 128],
                    lhsT=ident16[0:2, j:j + 1], rhs=statT2[:],
                    start=True, stop=True),
                    waits=[e_tT2, e_id16, e_se] if j == 0 else [])
            e_rw = sc.add("scalar", lambda h: h.copy(
                out=sx2row[:], in_=psS[0:1, 0:256]), waits=[e_g2])
            e_bm2 = sc.add("tensor", lambda h: h.matmul(
                out=psS[:, 0:256], lhsT=onesrow[:], rhs=sx2row[:],
                start=True, stop=True), waits=[e_rw])
            e_sxr = sc.add("scalar", lambda h: h.copy(
                out=sxr[:], in_=psS[:, 0:256]), waits=[e_bm2])
            qg_scale[g] = e_sxr
            e_se_prev = e_sxr
        psclose()
        ar.reset(mkB)
        B = sc.barrier()

        stkB = psopen()
        wq2sb = [ar.alloc([128, D], BF16) for _ in range(NHC)]
        hqs = [ar.alloc([128, NHC * GB], BF16) for _ in range(2)]
        syc = invs2c
        psY = [psum(stkB, [128, D], F32) for _ in range(2)]

        e_w2l = 0
        for m in range(NHC):
            e_w2l = sc.add("sync", lambda h, m=m: h.dma_start(
                out=wq2sb[m][:], in_=wq2d[m * 128:(m + 1) * 128, :]),
                waits=(B + [e_wq2]) if m == 0 else (), dma=True)

        e_res_hist = []
        e_gdone = []
        e_mm = 0
        for g in range(NGB):
            buf = hqs[g % 2]
            wt = list(B)
            if g >= 2:
                wt.append(e_gdone[g - 2])
            e_hl = 0
            for m in range(NHC):
                e_hl = sc.add("sync", lambda h, g=g, m=m, buf=buf: h.dma_start(
                    out=buf[:, m * GB:(m + 1) * GB],
                    in_=hqd[m, :, g * GB:(g + 1) * GB]),
                    waits=wt if m == 0 else (), dma=True)
            for ts in range(4):
                tch = g * 4 + ts
                ps = psY[ts % 2]
                wt2 = [e_hl, e_w2l]
                if len(e_res_hist) >= 2:
                    wt2.append(e_res_hist[-2])
                for m in range(NHC):
                    for eh in range(2):
                        e_mm = sc.add("tensor", lambda h, m=m, ts=ts, eh=eh, ps=ps, buf=buf: h.matmul(
                            out=ps[:, eh * 512:(eh + 1) * 512],
                            lhsT=buf[:, m * GB + ts * 128:m * GB + (ts + 1) * 128],
                            rhs=wq2sb[m][:, eh * 512:(eh + 1) * 512],
                            start=(m == 0), stop=(m == NHC - 1)),
                            waits=wt2 if (m == 0 and eh == 0) else [])
                e_res_hist.append(sc.add("vector", lambda h, tch=tch, ps=ps: h.scalar_tensor_tensor(
                    out=xt[:, tch * D:(tch + 1) * D], in0=ps[:, :],
                    scalar=syc[:, tch:tch + 1], op0=ALU.mult, op1=ALU.add,
                    in1=xt[:, tch * D:(tch + 1) * D]),
                    waits=[e_mm]))
            e_gdone.append(e_mm)
        psclose()
        ar.reset(mkF)
        sc.barrier()
        return sc.evt("vector")

    def att(e_xrdy):
        mkA = ar.mark()
        xhF = [ar.alloc([128, T], BF16) for _ in range(NDC)]
        e_lnq_s, _ = lnq(False, e_xrdy, xhF)

        stk = psopen()
        obuf = [ar.alloc([128, T], BF16) for _ in range(NDC)]
        mk2 = ar.mark()
        crep = ar.alloc([128, T], F32)
        srep = ar.alloc([128, T], F32)
        wkv = [ar.alloc([128, 3 * 256], BF16) for _ in range(NDC)]
        qkF = ar.alloc([128, T], F32)
        qswp = ar.alloc([128, T], F32)
        qhB = [ar.alloc([128, T], BF16) for _ in range(2)]
        khB = [ar.alloc([128, T], BF16) for _ in range(2)]
        vtok = ar.alloc([128, NKC * 132], BF16)
        expb = [ar.alloc([128, QHW], BF16) for _ in range(2)]
        drow = ar.alloc([1, QHW], F32)
        drep = ar.alloc([64, QHW], F32)
        rmst = qswp[0:32, :]
        psS_ = [psum(stk, [128, QHW], F32) for _ in range(2)]
        psO = [psum(stk, [128, QHW], F32) for _ in range(2)]

        e_rs = sc.add("sync", lambda h: h.dma_start(
            out=rmst, in_=roped[0]), waits=[e_ropew1], dma=True)
        for q in range(4):
            sc.add("vector", lambda h, q=q: h.tensor_copy(
                out=srep[32 * q:32 * (q + 1), :], in_=rmst),
                waits=[e_rs] if q == 0 else [])
        e_rc = sc.add("sync", lambda h: h.dma_start(
            out=rmst, in_=roped[1]),
            waits=[e_ropew2, sc.evt("vector")], dma=True)
        for q in range(4):
            sc.add("vector", lambda h, q=q: h.tensor_copy(
                out=crep[32 * q:32 * (q + 1), :], in_=rmst),
                waits=[e_rc] if q == 0 else [])
        sc.add("vector", lambda h: h.tensor_scalar(
            out=srep[0:32, :], in0=srep[0:32, :], scalar1=-1.0, scalar2=None,
            op0=ALU.mult))
        sc.add("vector", lambda h: h.tensor_scalar(
            out=srep[64:96, :], in0=srep[64:96, :], scalar1=-1.0, scalar2=None,
            op0=ALU.mult))
        e_tab = sc.evt("vector")
        e_von = sc.add("vector", lambda h: h.memset(
            vtok.ap().rearrange("p (k c) -> p k c", k=NKC)[:, :, 64:65], 1.0))
        e_von2 = sc.add("vector", lambda h: h.memset(
            vtok.ap().rearrange("p (k c) -> p k c", k=NKC)[:, :, 130:131], 1.0))

        e_on = 0
        for hg in range(NHG):
            Bh = sc.barrier()
            e_wl = 0
            for k in range(NDC):
                for part in range(3):
                    e_wl = sc.add("sync", lambda h, k=k, part=part, hg=hg: h.dma_start(
                        out=wkv[k][:, part * 256:(part + 1) * 256],
                        in_=wqkvd[k * 128:(k + 1) * 128,
                                  part * D + hg * 256:part * D + (hg + 1) * 256]),
                        waits=(Bh + [e_wqkv]) if (k == 0 and part == 0) else (),
                        dma=True)
            e_rope = 0
            e_s = 0
            for part in range(2):
                dstl = qhB if part == 0 else khB
                for mc in range(2):
                    e_ev = 0
                    for tg in range(T // 512):
                        ps = psS_[tg % 2]
                        wt = [e_wl, e_lnq_s] + Bh
                        if tg >= 2:
                            wt.append(e_ev)
                        if tg == 0 and e_rope:
                            wt.append(e_rope)
                            wt.append(e_s)
                        e_mm = 0
                        for k in range(NDC):
                            e_mm = sc.add("tensor", lambda h, k=k, part=part, mc=mc, tg=tg, ps=ps: h.matmul(
                                out=ps[:, 0:512],
                                lhsT=wkv[k][:, part * 256 + mc * 128:part * 256 + (mc + 1) * 128],
                                rhs=xhF[k][:, tg * 512:(tg + 1) * 512],
                                start=(k == 0), stop=(k == NDC - 1)),
                                waits=wt if k == 0 else [])
                        e_ev = sc.add("scalar", lambda h, tg=tg, ps=ps: h.copy(
                            out=qkF[:, tg * 512:(tg + 1) * 512], in_=ps[:, 0:512]),
                            waits=[e_mm])
                    for (a, b2) in ((0, 32), (32, 0), (64, 96), (96, 64)):
                        e_s = sc.add("gpsimd", lambda h, a=a, b2=b2: h.tensor_copy(
                            out=qswp[a:a + 32, :], in_=qkF[b2:b2 + 32, :]),
                            waits=[e_ev, e_tab])
                    sc.add("vector", lambda h: h.tensor_tensor(
                        out=qswp[:], in0=qswp[:], in1=srep[:], op=ALU.mult),
                        waits=[e_s, e_tab])
                    sc.add("vector", lambda h: h.tensor_tensor(
                        out=qkF[:], in0=qkF[:], in1=crep[:], op=ALU.mult),
                        waits=[e_ev])
                    e_rope = sc.add("vector", lambda h, mc=mc, dstl=dstl: h.tensor_tensor(
                        out=dstl[mc][:], in0=qkF[:], in1=qswp[:], op=ALU.add))
            e_qkv_ev = sc.evt("scalar")
            for pr in range(2):
                e_vt = 0
                e_vt_hist = {}
                for tt in range(NTC):
                    ps = psS_[tt % 2]
                    wt = [e_wl, e_qkv_ev,
                          e_rope if pr == 0 else e_on]
                    if pr == 0:
                        wt.append(e_von2)
                    if tt >= 2:
                        wt.append(e_vt_hist[tt - 2])
                    e_mm = 0
                    for k in range(NDC):
                        e_mm = sc.add("tensor", lambda h, k=k, tt=tt, pr=pr, ps=ps: h.matmul(
                            out=ps[:, 0:128],
                            lhsT=xhF[k][:, tt * 128:(tt + 1) * 128],
                            rhs=wkv[k][:, 512 + pr * 128:512 + (pr + 1) * 128],
                            start=(k == 0), stop=(k == NDC - 1)),
                            waits=wt if k == 0 else [])
                    sc.add("scalar", lambda h, tt=tt, ps=ps: h.copy(
                        out=vtok[:, tt * 132:tt * 132 + 64], in_=ps[:, 0:64]),
                        waits=[e_mm])
                    e_vt = sc.add("scalar", lambda h, tt=tt, ps=ps: h.copy(
                        out=vtok[:, tt * 132 + 66:tt * 132 + 130], in_=ps[:, 64:128]),
                        waits=[])
                    e_vt_hist[tt] = e_vt
                for qh in range(NQH):
                    e_exp = {}
                    e_pv = 0
                    e_pv_prev = 0
                    for kc in range(NKC):
                        wt = [e_vt, e_rope]
                        if kc > 0:
                            wt.append(e_exp[(kc - 1, 1)])
                        nqg = QHW // 512
                        if kc == 0 and (qh > 0 or pr > 0 or hg > 0):
                            wt.append(e_on)
                        e_smm = 0
                        for xy in range(2):
                            for qg in range(nqg):
                                e_smm = sc.add("tensor", lambda h, pr=pr, qh=qh, kc=kc, xy=xy, qg=qg: h.matmul(
                                    out=psS_[xy][:, qg * 512:(qg + 1) * 512],
                                    lhsT=khB[pr][xy * 64:(xy + 1) * 64,
                                                 kc * 128:(kc + 1) * 128],
                                    rhs=qhB[pr][xy * 64:(xy + 1) * 64,
                                                qh * QHW + qg * 512:qh * QHW + (qg + 1) * 512],
                                    start=True, stop=True,
                                    tile_position=(xy * 64, 0)),
                                    waits=wt if (xy == 0 and qg == 0) else [])
                        for xy in range(2):
                            wte = [e_smm]
                            if e_pv_prev:
                                wte.append(e_pv_prev)
                            e_exp[(kc, xy)] = sc.add("scalar", lambda h, xy=xy: h.activation(
                                out=expb[xy][:], in_=psS_[xy][:],
                                func=AF.Exp, scale=0.125, bias=C("neg15")),
                                waits=wte)
                        for xy in range(2):
                            for qg in range(nqg):
                                e_pv = sc.add("tensor", lambda h, xy=xy, qg=qg, kc=kc: h.matmul(
                                    out=psO[xy][0:65, qg * 512:(qg + 1) * 512],
                                    lhsT=vtok[:, kc * 132 + xy * 66:kc * 132 + xy * 66 + 65],
                                    rhs=expb[xy][:, qg * 512:(qg + 1) * 512],
                                    start=(kc == 0), stop=(kc == NKC - 1)),
                                    waits=[e_exp[(kc, xy)]] if qg == 0 else [])
                        e_pv_prev = e_pv
                    for xy in range(2):
                        head = 4 * hg + 2 * pr + xy
                        e_dr = sc.add("vector", lambda h, xy=xy: h.tensor_copy(
                            out=drow[:], in_=psO[xy][64:65, :]),
                            waits=[e_pv])
                        e_rcp = sc.add("vector", lambda h: h.reciprocal(
                            out=drow[:], in_=drow[:]))
                        e_dm = 0
                        for qg in range(QHW // 512):
                            wt5 = [e_rcp, e_ones, e_exp[(NKC - 1, xy)]]
                            if e_on:
                                wt5.append(e_on)
                            e_dm = sc.add("tensor", lambda h, xy=xy, qg=qg: h.matmul(
                                out=psS_[xy][0:64, qg * 512:(qg + 1) * 512],
                                lhsT=onesrow[0:1, 0:64],
                                rhs=drow[0:1, qg * 512:(qg + 1) * 512],
                                start=True, stop=True),
                                waits=wt5 if qg == 0 else [])
                        e_db = sc.add("scalar", lambda h, xy=xy: h.copy(
                            out=drep[:], in_=psS_[xy][0:64, :]), waits=[e_dm])
                        e_on = sc.add("vector", lambda h, xy=xy, head=head, qh=qh: h.tensor_tensor(
                            out=obuf[head // 2][(head % 2) * 64:(head % 2) * 64 + 64,
                                                qh * QHW:(qh + 1) * QHW],
                            in0=psO[xy][0:64, :], in1=drep[:], op=ALU.mult),
                            waits=[e_db])
        Bo = sc.barrier()
        ar.reset(mk2)
        psX = psS_ if QHW == D else [psum(stk, [128, D], F32) for _ in range(2)]
        wob = [ar.alloc([128, D], BF16) for _ in range(NDC)]
        e_wol = 0
        for k in range(NDC):
            e_wol = sc.add("sync", lambda h, k=k: h.dma_start(
                out=wob[k][:], in_=wod[k * 128:(k + 1) * 128, :]),
                waits=(Bo + [e_wo]) if k == 0 else (), dma=True)
        e_resh = []
        for i in range(NTC):
            ps = psX[i % 2]
            wt = [e_wol, e_on]
            if len(e_resh) >= 2:
                wt.append(e_resh[-2])
            e_mm = 0
            for k in range(NDC):
                for eh in range(2):
                    e_mm = sc.add("tensor", lambda h, i=i, k=k, eh=eh, ps=ps: h.matmul(
                        out=ps[:, eh * 512:(eh + 1) * 512],
                        lhsT=obuf[k][:, i * 128:(i + 1) * 128],
                        rhs=wob[k][:, eh * 512:(eh + 1) * 512],
                        start=(k == 0), stop=(k == NDC - 1)),
                        waits=wt if (k == 0 and eh == 0) else [])
            e_resh.append(sc.add("vector", lambda h, i=i, ps=ps: h.tensor_tensor(
                out=xt[:, i * D:(i + 1) * D], in0=ps[:, :],
                in1=xt[:, i * D:(i + 1) * D], op=ALU.add),
                waits=[e_mm]))
        psclose()
        ar.reset(mkA)
        sc.barrier()
        return sc.evt("vector")

    e_store = 0
    e_xrdy = e_xload
    for b in range(2):
        ffn(e_xrdy)
        att(e_xrdy)
        e_done = ffn(e_xrdy)
        for i in range(NTC):
            e_store = sc.add("gpsimd", lambda h, b=b, i=i: h.dma_start(
                out=x_out[b, i * 128:(i + 1) * 128, :],
                in_=xt[:, i * D:(i + 1) * D]),
                waits=[e_done] if i == 0 else (), dma=True)
        if b == 0:
            e_xrdy = load_x(1, waits=[e_store])

    sc.add("sync", "SEMINC", waits=[e_store])

    with ExitStack() as stk:
        sems = {k: stk.enter_context(nc.semaphore(f"sem_{k}")) for k in Sched.KEYS}
        block = stk.enter_context(nc.Block())
        sc.emit(block, sems)
    return nc


_NC_CACHE = [None]


def kernel(L0_attn_x, L0_attn_w_qkv, L0_attn_w_o, L0_ffn_w1, L0_ffn_conv_w,
           L0_ffn_conv_b, L0_ffn_w2):
    x = np.ascontiguousarray(L0_attn_x, dtype=np.float32)
    wqkv = np.ascontiguousarray(L0_attn_w_qkv, dtype=np.float32)
    wo = np.ascontiguousarray(L0_attn_w_o, dtype=np.float32)
    w1t = np.ascontiguousarray(np.asarray(L0_ffn_w1).T, dtype=np.float32)
    w2t = np.ascontiguousarray(np.asarray(L0_ffn_w2).T, dtype=np.float32)
    cwv = np.ascontiguousarray(
        np.asarray(L0_ffn_conv_w).reshape(NHC, 128, 3).transpose(1, 0, 2)
        .reshape(128, NHC * 3), dtype=np.float32)
    cbv = np.ascontiguousarray(
        np.asarray(L0_ffn_conv_b).reshape(NHC, 128).T, dtype=np.float32)
    ident = np.eye(128, dtype=ml_dtypes.bfloat16)
    ident16 = np.eye(16, dtype=np.float32)
    consts = np.zeros((128, 8), np.float32)
    consts[:, CN["half"]] = 0.5
    consts[:, CN["neghalf"]] = -0.5
    consts[:, CN["neg15"]] = -15.0
    consts[:, CN["one"]] = 1.0

    if _NC_CACHE[0] is None:
        _NC_CACHE[0] = build_program()
    nc = _NC_CACHE[0]

    in_maps = []
    for c in range(8):
        in_maps.append({
            "x_in": x[2 * c:2 * c + 2],
            "w_qkv": wqkv, "w_o": wo, "w1t": w1t, "w2t": w2t,
            "cwt": cwv, "cbt": cbv, "identin": ident, "ident16in": ident16,
            "constin": consts,
        })
    res = run_bass_kernel_spmd(nc, in_maps, core_ids=list(range(8)))
    out = np.concatenate([r["x_out"] for r in res.results], axis=0)
    return np.ascontiguousarray(out, dtype=np.float32)


# version 10 (final)
# speedup vs baseline: 1.1773x; 1.0099x over previous
import math
import numpy as np
import ml_dtypes
from contextlib import ExitStack

import concourse.bass as bass
import concourse.mybir as mybir
from concourse.bass_utils import run_bass_kernel_spmd

F32 = mybir.dt.float32
BF16 = mybir.dt.bfloat16
I32 = mybir.dt.int32
AF = mybir.ActivationFunctionType
ALU = mybir.AluOpType
AX = mybir.AxisListType

T = 2048
D = 1024
HID = 4096
NTC = T // 128
NDC = D // 128
NHC = HID // 128
GA = 256
NGA = T // GA
GB = 512
NGB = T // GB
NHG = 4
NKC = T // 128
QHW = 1024 if T >= 1024 else T
NQH = T // QHW

SBUF_BASE = 18432
SBUF_CAP = 229376

_TWO_PI = 2.0 * math.pi
_C1 = math.floor(_TWO_PI * 512.0) / 512.0
_C2 = float(np.float32(math.floor((_TWO_PI - _C1) * (1 << 22)) / (1 << 22)))
_C3 = _TWO_PI - _C1 - _C2
_PI_F32 = float(np.float32(math.pi))
_PIH_HI = float(np.float32(math.pi / 2))
_PIH_LO = math.pi / 2 - _PIH_HI
_LN_HI = float(np.float32(math.log(10000.0)))
_LN_LO = math.log(10000.0) - _LN_HI

CN = {"half": 0, "neghalf": 1, "neg15": 2, "one": 3}


def _set_T(newT):
    global T, NTC, NGA, NGB, NKC, QHW, NQH
    T = newT
    NTC = T // 128
    NGA = T // GA
    NGB = T // GB
    NKC = T // 128
    QHW = 1024 if T >= 1024 else T
    NQH = T // QHW


class Sched:
    ENGINES = ("tensor", "vector", "scalar", "gpsimd", "sync")
    DKEY = {"sync": "syncd", "gpsimd": "gpsd", "vector": "vecd",
            "scalar": "scad", "tensor": "tend"}
    KEYS = ENGINES + ("syncd", "gpsd", "vecd", "scad", "tend")

    def __init__(self):
        self.steps = {e: [] for e in self.ENGINES}
        self.cnt = {k: 0 for k in self.KEYS}

    def add(self, eng, fn, waits=(), inc=True, dma=False):
        w = []
        for it in waits:
            if not it:
                continue
            k, v = it
            if k != eng and v > 0:
                w.append((k, v))
        self.steps[eng].append((fn, w, inc, dma))
        key = self.DKEY[eng] if dma else eng
        if inc:
            self.cnt[key] += 16 if dma else 1
        return (key, self.cnt[key])

    def evt(self, eng):
        return (eng, self.cnt[eng])

    def snap(self):
        return [(k, self.cnt[k]) for k in self.KEYS]

    def barrier(self):
        s = self.snap()
        for e in self.ENGINES:
            self.add(e, "SEMINC", waits=s)
        return self.snap()

    def emit(self, block, sems):
        for eng in self.ENGINES:
            steps = self.steps[eng]
            if not steps:
                continue

            def body(h, steps=steps, eng=eng):
                hi = {k: 0 for k in self.KEYS}
                dkey = self.DKEY[eng]
                done = 0
                for fn, waits, inc, dma in steps:
                    if eng == "vector" and done > hi[eng]:
                        h.wait_ge(sems[eng], done)
                        hi[eng] = done
                    for (wk, wv) in waits:
                        if wv > hi[wk]:
                            h.wait_ge(sems[wk], wv)
                            hi[wk] = wv
                    if fn == "SEMINC":
                        h.sem_inc(sems[eng], 1)
                        done += 1
                        continue
                    ins = fn(h)
                    if inc:
                        ins.then_inc(sems[dkey if dma else eng], 16 if dma else 1)
                        if not dma:
                            done += 1

            getattr(block, eng)(body)


class Arena:
    def __init__(self, nc):
        self.nc = nc
        self.off = SBUF_BASE
        self.n = 0
        self.peak = 0

    def alloc(self, shape, dtype, align=64, at=None):
        nbytes = int(np.prod(shape[1:])) * mybir.dt.size(dtype)
        if at is None:
            self.off = (self.off + align - 1) // align * align
            off = self.off
            self.off += nbytes
            self.peak = max(self.peak, self.off)
            assert self.off <= SBUF_CAP, f"SBUF overflow: {self.off}"
        else:
            off = at
        t = self.nc.alloc_sbuf_tensor_at(
            f"sb{self.n}", list(shape), dtype, offset=off)
        self.n += 1
        self.last = off
        return t

    def mark(self):
        return self.off

    def reset(self, mark):
        self.off = mark


def build_program():
    nc = bass.Bass("TRN2", target_bir_lowering=False, debug=False,
                   detect_race_conditions=False)

    x_in = nc.dram_tensor("x_in", [2, T, D], F32, kind="ExternalInput").ap()
    w_qkv = nc.dram_tensor("w_qkv", [D, 3 * D], F32, kind="ExternalInput").ap()
    w_o = nc.dram_tensor("w_o", [D, D], F32, kind="ExternalInput").ap()
    w1t_d = nc.dram_tensor("w1t", [D, HID], F32, kind="ExternalInput").ap()
    w2t_d = nc.dram_tensor("w2t", [HID, D], F32, kind="ExternalInput").ap()
    cwt = nc.dram_tensor("cwt", [128, 3 * NHC], F32, kind="ExternalInput").ap()
    cbt = nc.dram_tensor("cbt", [128, NHC], F32, kind="ExternalInput").ap()
    identin = nc.dram_tensor("identin", [128, 128], BF16, kind="ExternalInput").ap()
    ident16in = nc.dram_tensor("ident16in", [16, 16], F32, kind="ExternalInput").ap()
    constin = nc.dram_tensor("constin", [128, 8], F32, kind="ExternalInput").ap()
    x_out = nc.dram_tensor("x_out", [2, T, D], F32, kind="ExternalOutput").ap()

    wq1d = nc.dram_tensor("wq1d", [D, HID], BF16, kind="Internal").ap()
    wq2d = nc.dram_tensor("wq2d", [HID, D], BF16, kind="Internal").ap()
    wqkvd = nc.dram_tensor("wqkvd", [D, 3 * D], BF16, kind="Internal").ap()
    wod = nc.dram_tensor("wod", [D, D], BF16, kind="Internal").ap()
    hqd = nc.dram_tensor("hqd", [NHC, 128, T], BF16, kind="Internal").ap()
    statd = nc.dram_tensor("statd", [2 * T], F32, kind="Internal").ap()
    roped = nc.dram_tensor("roped", [2, 32, T], F32, kind="Internal").ap()

    sc = Sched()
    ar = Arena(nc)
    psum_stk = []

    def psopen():
        psum_stk.append(ExitStack())
        return psum_stk[-1]

    def psclose():
        psum_stk.pop().close()

    uid = [0]

    def psum(stk, shape, dtype=F32):
        uid[0] += 1
        return stk.enter_context(nc.psum_tensor(f"ps{uid[0]}", list(shape), dtype))

    xt = ar.alloc([128, NTC * D], F32)
    consts = ar.alloc([128, 8], F32)
    ident = ar.alloc([128, 128], BF16)
    ident16 = ar.alloc([16, 16], F32)
    cw = ar.alloc([128, 3 * NHC], F32)
    cb = ar.alloc([128, NHC], F32)
    smallv = ar.alloc([128, 4], F32)
    onesrow = ar.alloc([1, 128], F32)
    RESIDENT = ar.mark()

    def C(name):
        return consts[:, CN[name]:CN[name] + 1]

    def fp32_transpose(srcap, np_, nf, dstap, scr_f32, scr_bf, psT3, waits):
        ev_ad = None
        for part in range(3):
            if part == 0:
                e_c = sc.add("vector", lambda h: h.tensor_copy(
                    out=scr_bf, in_=srcap), waits=waits)
                e_d = sc.add("vector", lambda h: h.tensor_tensor(
                    out=scr_f32, in0=srcap, in1=scr_bf, op=ALU.subtract))
            else:
                e_c = sc.add("vector", lambda h: h.tensor_copy(
                    out=scr_bf, in_=scr_f32), waits=[ev_ad] if ev_ad else [])
                e_d = sc.add("vector", lambda h: h.tensor_tensor(
                    out=scr_f32, in0=scr_f32, in1=scr_bf, op=ALU.subtract))
            e_t = sc.add("tensor", lambda h: h.transpose(
                out=psT3[0:nf, 0:np_], in_=scr_bf, identity=ident[:]),
                waits=[e_c, e_ident])
            if part == 0:
                ev_ad = sc.add("scalar", lambda h: h.copy(
                    out=dstap, in_=psT3[0:nf, 0:np_]), waits=[e_t])
            else:
                ev_ad = sc.add("vector", lambda h: h.tensor_tensor(
                    out=dstap, in0=dstap, in1=psT3[0:nf, 0:np_], op=ALU.add),
                    waits=[e_t, ev_ad])
        return ev_ad

    e_const = sc.add("sync", lambda h: h.dma_start(out=consts[:], in_=constin[:]), dma=True)
    e_ident = sc.add("sync", lambda h: h.dma_start(out=ident[:], in_=identin[:]), dma=True)
    e_id16 = sc.add("sync", lambda h: h.dma_start(out=ident16[:], in_=ident16in[:]), dma=True)
    e_cw = sc.add("sync", lambda h: h.dma_start(out=cw[:], in_=cwt[:]), dma=True)
    e_cb = sc.add("sync", lambda h: h.dma_start(out=cb[:], in_=cbt[:]), dma=True)
    e_ones = sc.add("vector", lambda h: h.memset(onesrow[:], 1.0))

    def load_x(b, waits=()):
        ev = None
        for i in range(NTC):
            ev = sc.add("gpsimd", lambda h, i=i, b=b: h.dma_start(
                out=xt[:, i * D:(i + 1) * D],
                in_=x_in[b, i * 128:(i + 1) * 128, :]),
                waits=waits if i == 0 else (), dma=True)
        return ev

    e_xload = load_x(0)

    m0 = ar.mark()
    ps0 = psopen()
    tvec = ar.alloc([1, T], F32)
    iv1 = ar.alloc([1, 32], F32)
    iv2 = ar.alloc([1, 32], F32)
    ftab = ar.alloc([32, T], F32)
    rtab = ar.alloc([32, T], F32)
    ntab = ar.alloc([32, T], F32)
    gtab = ar.alloc([32, T], F32)
    ntabi = ar.alloc([32, T], I32)
    sinM = ar.alloc([32, T], F32)
    cosM = ar.alloc([32, T], F32)
    psF = psum(ps0, [32, 512], F32)

    e_tv = sc.add("gpsimd", lambda h: h.iota(
        tvec[:], pattern=[[1, T]], base=0, channel_multiplier=0,
        allow_small_or_imprecise_dtypes=True))
    e_i1 = sc.add("gpsimd", lambda h: h.iota(
        iv1[:], pattern=[[1, 32]], base=0, channel_multiplier=0,
        allow_small_or_imprecise_dtypes=True))
    e_i2 = sc.add("gpsimd", lambda h: h.iota(
        iv2[:], pattern=[[1, 32]], base=0, channel_multiplier=0,
        allow_small_or_imprecise_dtypes=True))
    sc.add("vector", lambda h: h.tensor_scalar(
        out=iv1[:], in0=iv1[:], scalar1=-_LN_HI / 32.0, scalar2=None, op0=ALU.mult),
        waits=[e_i1])
    e_v2 = sc.add("vector", lambda h: h.scalar_tensor_tensor(
        out=iv1[:], in0=iv2[:], scalar=-_LN_LO / 32.0, op0=ALU.mult, op1=ALU.add,
        in1=iv1[:]), waits=[e_i2])
    e_exp = sc.add("scalar", lambda h: h.activation(
        out=iv1[:], in_=iv1[:], func=AF.Exp), waits=[e_v2])
    e_cp = 0
    for tg in range(T // 512):
        wt = [e_exp, e_tv]
        if tg:
            wt.append(e_cp)
        e_mmf = sc.add("tensor", lambda h, tg=tg: h.matmul(
            out=psF[:, :], lhsT=iv1[:], rhs=tvec[:, tg * 512:(tg + 1) * 512],
            start=True, stop=True), waits=wt)
        e_cp = sc.add("vector", lambda h, tg=tg: h.tensor_copy(
            out=ftab[:, tg * 512:(tg + 1) * 512], in_=psF[:, :]),
            waits=[e_mmf])
    sc.add("vector", lambda h: h.tensor_scalar(
        out=ntab[:], in0=ftab[:], scalar1=1.0 / _TWO_PI, scalar2=None, op0=ALU.mult))
    sc.add("vector", lambda h: h.tensor_scalar(
        out=ntabi[:], in0=ntab[:], scalar1=0.0, scalar2=None, op0=ALU.add))
    sc.add("vector", lambda h: h.tensor_copy(out=gtab[:], in_=ntabi[:]))
    sc.add("vector", lambda h: h.tensor_tensor(
        out=ntab[:], in0=gtab[:], in1=ntab[:], op=ALU.is_gt))
    sc.add("vector", lambda h: h.tensor_tensor(
        out=ntab[:], in0=gtab[:], in1=ntab[:], op=ALU.subtract))
    for c in (_C1, _C2, _C3):
        sc.add("vector", lambda h, c=c: h.scalar_tensor_tensor(
            out=rtab[:] if c != _C1 else rtab[:], in0=ntab[:], scalar=-c,
            op0=ALU.mult, op1=ALU.add, in1=ftab[:] if c == _C1 else rtab[:]))
    sc.add("vector", lambda h: h.tensor_scalar(
        out=gtab[:], in0=rtab[:], scalar1=_PI_F32, scalar2=None, op0=ALU.is_gt))
    for c in (_C1, _C2, _C3):
        sc.add("vector", lambda h, c=c: h.scalar_tensor_tensor(
            out=rtab[:], in0=gtab[:], scalar=-c, op0=ALU.mult, op1=ALU.add,
            in1=rtab[:]))
    e_cl = sc.add("vector", lambda h: h.tensor_scalar(
        out=rtab[:], in0=rtab[:], scalar1=_PI_F32, scalar2=-_PI_F32,
        op0=ALU.min, op1=ALU.max))
    e_sin = sc.add("scalar", lambda h: h.activation(
        out=sinM[:], in_=rtab[:], func=AF.Sin), waits=[e_cl])
    e_abs = sc.add("scalar", lambda h: h.activation(
        out=rtab[:], in_=rtab[:], func=AF.Abs))
    sc.add("vector", lambda h: h.tensor_scalar(
        out=rtab[:], in0=rtab[:], scalar1=-1.0, scalar2=_PIH_HI,
        op0=ALU.mult, op1=ALU.add), waits=[e_abs])
    e_r2 = sc.add("vector", lambda h: h.tensor_scalar(
        out=rtab[:], in0=rtab[:], scalar1=_PIH_LO, scalar2=None, op0=ALU.add))
    e_cos = sc.add("scalar", lambda h: h.activation(
        out=cosM[:], in_=rtab[:], func=AF.Sin), waits=[e_r2])
    e_ropew1 = sc.add("gpsimd", lambda h: h.dma_start(
        out=roped[0], in_=sinM[:]), waits=[e_sin], dma=True)
    e_ropew2 = sc.add("gpsimd", lambda h: h.dma_start(
        out=roped[1], in_=cosM[:]), waits=[e_cos], dma=True)
    psclose()

    ps0 = psopen()
    wbuf = [ar.alloc([128, HID], F32) for _ in range(2)]
    wabs = ar.alloc([128, HID], F32)
    wsgn = ar.alloc([128, HID], BF16)
    wqb = ar.alloc([128, HID], BF16)
    wsum = ar.alloc([128, 8], F32)
    wtot = ar.alloc([128, 1], F32)
    w11 = ar.alloc([1, 2], F32)
    psW = psum(ps0, [1, 2], F32)
    psW2 = psum(ps0, [128, 2], F32)

    def wquant(load_chunk, store_chunk, nch, ncol, nelem, sw_col, invsw_col):
        ev_red = []
        e_l = 0
        for c in range(nch):
            buf = wbuf[c % 2]
            wprev = [ev_red[c - 2]] if c >= 2 else []
            e_l = load_chunk(c, buf, wprev)
            ev_red.append(sc.add("vector", lambda h, c=c, buf=buf: h.tensor_reduce(
                out=wsum[:, c:c + 1], in_=buf[:, 0:ncol], axis=AX.X,
                op=ALU.add, apply_absolute_value=True), waits=[e_l]))
        e_t = sc.add("vector", lambda h: h.tensor_reduce(
            out=wtot[:], in_=wsum[:, 0:nch], axis=AX.X, op=ALU.add))
        e_mm = sc.add("tensor", lambda h: h.matmul(
            out=psW[0:1, 0:1], lhsT=wtot[:], rhs=C("one"), start=True, stop=True),
            waits=[e_t, e_const])
        sc.add("vector", lambda h: h.tensor_copy(out=w11[0:1, 0:1], in_=psW[0:1, 0:1]),
               waits=[e_mm])
        sc.add("vector", lambda h: h.tensor_scalar(
            out=w11[0:1, 0:1], in0=w11[0:1, 0:1], scalar1=1.0 / nelem,
            scalar2=1e-5, op0=ALU.mult, op1=ALU.max))
        e_c3 = sc.add("vector", lambda h: h.reciprocal(
            out=w11[0:1, 1:2], in_=w11[0:1, 0:1]))
        e_bm = sc.add("tensor", lambda h: h.matmul(
            out=psW2[:, :], lhsT=onesrow[:], rhs=w11[:], start=True, stop=True),
            waits=[e_c3, e_ones])
        e_bc = sc.add("vector", lambda h, sw_col=sw_col, invsw_col=invsw_col: h.tensor_copy(
            out=smallv[:, invsw_col:invsw_col + 1], in_=psW2[:, 0:1]),
            waits=[e_bm])
        e_bc = sc.add("vector", lambda h, sw_col=sw_col: h.tensor_copy(
            out=smallv[:, sw_col:sw_col + 1], in_=psW2[:, 1:2]))
        ev_q = []
        e_st = 0
        for c in range(nch):
            buf = wbuf[c % 2]
            wprev = [ev_q[c - 2]] if c >= 2 else [ev_red[-1]]
            e_l = load_chunk(c, buf, wprev)
            e_a = sc.add("scalar", lambda h, buf=buf: h.activation(
                out=wabs[:, 0:ncol], in_=buf[:, 0:ncol], func=AF.Abs,
                scale=smallv[:, sw_col:sw_col + 1]),
                waits=[e_l, e_bc])
            e_s = sc.add("scalar", lambda h, buf=buf: h.activation(
                out=wsgn[:, 0:ncol], in_=buf[:, 0:ncol], func=AF.Sign))
            sc.add("vector", lambda h: h.tensor_scalar(
                out=wabs[:, 0:ncol], in0=wabs[:, 0:ncol], scalar1=0.5,
                scalar2=None, op0=ALU.is_ge), waits=[e_a])
            e_q = sc.add("vector", lambda h: h.tensor_tensor(
                out=wqb[:, 0:ncol], in0=wabs[:, 0:ncol], in1=wsgn[:, 0:ncol],
                op=ALU.mult), waits=[e_s])
            ev_q.append(e_q)
            e_st = store_chunk(c, e_q)
        return e_st

    def w1_load(c, buf, wprev):
        return sc.add("sync", lambda h, c=c, buf=buf: h.dma_start(
            out=buf[:, 0:HID], in_=w1t_d[c * 128:(c + 1) * 128, :]),
            waits=wprev, dma=True)

    def w1_store(c, e_q):
        return sc.add("sync", lambda h, c=c: h.dma_start(
            out=wq1d[c * 128:(c + 1) * 128, :], in_=wqb[:, 0:HID]),
            waits=[e_q], dma=True)

    e_wq1 = wquant(w1_load, w1_store, NDC, HID, D * HID, sw_col=0, invsw_col=1)

    def w2_load(c, buf, wprev):
        ev = None
        for j in range(4):
            ev = sc.add("sync", lambda h, c=c, j=j, buf=buf: h.dma_start(
                out=buf[:, j * D:(j + 1) * D],
                in_=w2t_d[c * 512 + j * 128:c * 512 + (j + 1) * 128, :]),
                waits=wprev if j == 0 else (), dma=True)
        return ev

    def w2_store(c, e_q):
        ev = None
        for j in range(4):
            ev = sc.add("sync", lambda h, c=c, j=j: h.dma_start(
                out=wq2d[c * 512 + j * 128:c * 512 + (j + 1) * 128, :],
                in_=wqb[:, j * D:(j + 1) * D]),
                waits=[e_q] if j == 0 else (), dma=True)
        return ev

    e_wq2 = wquant(w2_load, w2_store, 8, HID, D * HID, sw_col=2, invsw_col=3)

    def wcast(src, dst, nch, ncol):
        ev = None
        evq_h = []
        for c in range(nch):
            buf = wbuf[c % 2]
            wprev = [evq_h[c - 2]] if c >= 2 else []
            e_l = sc.add("sync", lambda h, c=c, buf=buf: h.dma_start(
                out=buf[:, 0:ncol], in_=src[c * 128:(c + 1) * 128, :]),
                waits=wprev, dma=True)
            e_q = sc.add("vector", lambda h, buf=buf: h.tensor_copy(
                out=wqb[:, 0:ncol], in_=buf[:, 0:ncol]), waits=[e_l])
            evq_h.append(e_q)
            ev = sc.add("sync", lambda h, c=c: h.dma_start(
                out=dst[c * 128:(c + 1) * 128, :], in_=wqb[:, 0:ncol]),
                waits=[e_q], dma=True)
        return ev

    e_wqkv = wcast(w_qkv, wqkvd, NDC, 3 * D)
    e_wo = wcast(w_o, wod, NDC, D)
    psclose()
    ar.reset(m0)

    sc.barrier()

    def lnq(want_xq, e_xrdy, outF):
        mk = ar.mark()
        stk = psopen()
        red1 = ar.alloc([128, NTC], F32)
        red2 = ar.alloc([128, NTC], F32)
        rstd = ar.alloc([128, NTC], F32)
        nmr = ar.alloc([128, NTC], F32)
        sxc = ar.alloc([128, NTC], F32)
        invs = ar.alloc([128, NTC], F32)
        sqs = ar.alloc([128, D], F32)
        xhat = [ar.alloc([128, D], F32) for _ in range(2)]
        i32s = [ar.alloc([128, D], I32) for _ in range(2)]
        ihalf = [ar.alloc([128, D], BF16) for _ in range(2)]
        fixb = ar.alloc([128, D], BF16)
        xqT = [ar.alloc([128, D], BF16) for _ in range(2)]
        invsbf = ar.alloc([128, NTC], BF16)
        invsd = ar.alloc([128, NTC], F32)
        statT = ar.alloc([16, 128], F32)
        lnrow = ar.alloc([1, T], F32) if want_xq else None
        psT = [psum(stk, [128, 512], BF16) for _ in range(2)]
        psG = psum(stk, [1, NTC * 128], F32) if want_xq else None

        for i in range(NTC):
            xi = xt[:, i * D:(i + 1) * D]
            sc.add("vector", lambda h, xi=xi, i=i: h.tensor_reduce(
                out=red1[:, i:i + 1], in_=xi, axis=AX.X, op=ALU.add),
                waits=[e_xrdy] if i == 0 else [])
            sc.add("scalar", lambda h, xi=xi, i=i: h.activation(
                out=sqs[:], in_=xi, func=AF.Square, accum_out=red2[:, i:i + 1]),
                waits=[e_xrdy] if i == 0 else [])
        e_sq = sc.evt("scalar")
        sc.add("vector", lambda h: h.tensor_scalar(
            out=red1[:], in0=red1[:], scalar1=1.0 / D, scalar2=None, op0=ALU.mult),
            waits=[e_sq])
        sc.add("vector", lambda h: h.tensor_scalar(
            out=red2[:], in0=red2[:], scalar1=1.0 / D, scalar2=None, op0=ALU.mult))
        sc.add("vector", lambda h: h.tensor_tensor(
            out=rstd[:], in0=red1[:], in1=red1[:], op=ALU.mult))
        sc.add("vector", lambda h: h.tensor_tensor(
            out=red2[:], in0=red2[:], in1=rstd[:], op=ALU.subtract))
        e_ve = sc.add("vector", lambda h: h.tensor_scalar(
            out=red2[:], in0=red2[:], scalar1=1e-5, scalar2=None, op0=ALU.add))
        e_sd = sc.add("scalar", lambda h: h.activation(
            out=rstd[:], in_=red2[:], func=AF.Sqrt), waits=[e_ve])
        sc.add("vector", lambda h: h.reciprocal(out=rstd[:], in_=rstd[:]),
               waits=[e_sd])
        sc.add("vector", lambda h: h.tensor_tensor(
            out=nmr[:], in0=rstd[:], in1=rstd[:], op=ALU.mult))
        sc.add("vector", lambda h: h.tensor_tensor(
            out=nmr[:], in0=nmr[:], in1=red2[:], op=ALU.mult))
        sc.add("vector", lambda h: h.tensor_scalar(
            out=nmr[:], in0=nmr[:], scalar1=-0.5, scalar2=1.5,
            op0=ALU.mult, op1=ALU.add))
        sc.add("vector", lambda h: h.tensor_tensor(
            out=rstd[:], in0=rstd[:], in1=nmr[:], op=ALU.mult))
        sc.add("vector", lambda h: h.tensor_tensor(
            out=nmr[:], in0=red1[:], in1=rstd[:], op=ALU.mult))
        e_b = sc.add("vector", lambda h: h.tensor_scalar(
            out=nmr[:], in0=nmr[:], scalar1=-1.0, scalar2=None, op0=ALU.mult))

        e_tr = []
        e_q_hist = {}
        e_i32_hist = {}
        e_trT_hist = {}
        for i in range(NTC):
            xi = xt[:, i * D:(i + 1) * D]
            xh = xhat[i % 2]
            i32 = i32s[i % 2]
            ih = ihalf[i % 2]
            xq = xqT[i % 2]
            if want_xq:
                wt = [e_b] if i == 0 else []
                if i >= 2:
                    wt.append(e_i32_hist[i - 2])
                sc.add("vector", lambda h, xi=xi, xh=xh, i=i: h.tensor_scalar(
                    out=xh[:], in0=xi, scalar1=rstd[:, i:i + 1],
                    scalar2=nmr[:, i:i + 1], op0=ALU.mult, op1=ALU.add), waits=wt)
                sc.add("vector", lambda h, xh=xh, i=i: h.tensor_reduce(
                    out=sxc[:, i:i + 1], in_=xh[:], axis=AX.X, op=ALU.max,
                    apply_absolute_value=True))
                sc.add("vector", lambda h, i=i: h.tensor_scalar(
                    out=sxc[:, i:i + 1], in0=sxc[:, i:i + 1], scalar1=1e-5,
                    scalar2=None, op0=ALU.max))
                sc.add("vector", lambda h, i=i: h.tensor_scalar(
                    out=invs[:, i:i + 1], in0=sxc[:, i:i + 1],
                    scalar1=smallv[:, 1:2], scalar2=1.0 / 127.0,
                    op0=ALU.mult, op1=ALU.mult))
                sc.add("vector", lambda h, i=i: h.reciprocal(
                    out=sxc[:, i:i + 1], in_=sxc[:, i:i + 1]))
                e_sx = sc.add("vector", lambda h, i=i: h.tensor_scalar(
                    out=sxc[:, i:i + 1], in0=sxc[:, i:i + 1], scalar1=127.0,
                    scalar2=None, op0=ALU.mult))
                wt2 = [e_sx]
                if i >= 2:
                    wt2.append(e_q_hist[i - 2])
                e_i32_hist[i] = sc.add("scalar", lambda h, xh=xh, i32=i32, i=i: h.activation(
                    out=i32[:], in_=xh[:], func=AF.Identity,
                    scale=sxc[:, i:i + 1], bias=C("half")), waits=wt2)
                e_ih = sc.add("scalar", lambda h, i32=i32, ih=ih: h.activation(
                    out=ih[:], in_=i32[:], func=AF.Identity, bias=C("neghalf")))
                sc.add("vector", lambda h, xh=xh, ih=ih, i=i: h.scalar_tensor_tensor(
                    out=fixb[:], in0=xh[:], scalar=sxc[:, i:i + 1], op0=ALU.mult,
                    op1=ALU.is_lt, in1=ih[:]), waits=[e_ih])
                wtq = []
                if i >= 2:
                    wtq.append(e_trT_hist[i - 2])
                e_src = sc.add("vector", lambda h, ih=ih, xq=xq: h.scalar_tensor_tensor(
                    out=xq[:], in0=ih[:], scalar=0.5, op0=ALU.add,
                    op1=ALU.subtract, in1=fixb[:]), waits=wtq)
                e_q_hist[i] = e_src
            else:
                wt = [e_b] if i == 0 else []
                if i >= 2:
                    wt.append(e_trT_hist[i - 2])
                e_src = sc.add("vector", lambda h, xi=xi, xq=xq, i=i: h.tensor_scalar(
                    out=xq[:], in0=xi, scalar1=rstd[:, i:i + 1],
                    scalar2=nmr[:, i:i + 1], op0=ALU.mult, op1=ALU.add), waits=wt)
                e_q_hist[i] = e_src
            for k in range(NDC):
                n = i * NDC + k
                ps = psT[n % 2]
                wt3 = [e_src, e_ident]
                if n >= 2:
                    wt3.append(e_tr[n - 2])
                e_t = sc.add("tensor", lambda h, xq=xq, k=k, ps=ps: h.transpose(
                    out=ps[:, 0:128], in_=xq[:, k * 128:(k + 1) * 128],
                    identity=ident[:]), waits=wt3)
                e_trT_hist[i] = e_t
                e_tr.append(sc.add("scalar", lambda h, i=i, k=k, ps=ps: h.copy(
                    out=outF[k][:, i * 128:(i + 1) * 128], in_=ps[:, 0:128]),
                    waits=[e_t]))
        e_s_done = sc.evt("scalar")
        e_statw = None
        if want_xq:
            e_tT = fp32_transpose(invs[:], 128, NTC, statT[0:NTC, :],
                                  invsd[:], invsbf[:], psT[0],
                                  [sc.evt("vector"), e_tr[-1]])
            e_gm = 0
            for i in range(NTC):
                e_gm = sc.add("tensor", lambda h, i=i: h.matmul(
                    out=psG[0:1, i * 128:(i + 1) * 128],
                    lhsT=ident16[0:NTC, i:i + 1], rhs=statT[0:NTC, :],
                    start=True, stop=True),
                    waits=[e_tT, e_id16] if i == 0 else [])
            e_ev = sc.add("scalar", lambda h: h.copy(
                out=lnrow[:], in_=psG[:]), waits=[e_gm])
            e_statw = sc.add("gpsimd", lambda h: h.dma_start(
                out=statd.rearrange("(a t) -> a t", a=1)[:, 0:T], in_=lnrow[:]),
                waits=[e_ev], dma=True)
        sc.barrier()
        psclose()
        ar.reset(mk)
        return e_s_done, e_statw

    def ffn(e_xrdy):
        mkF = ar.mark()
        invs2c = ar.alloc([128, NTC], F32)
        mkB = ar.mark()
        xqF = [ar.alloc([128, T], BF16) for _ in range(NDC)]
        e_lnq_s, e_statw = lnq(True, e_xrdy, xqF)

        mkA = ar.mark()
        stkA = psopen()
        s1row = ar.alloc([1, 258], F32)
        s1row_off = ar.last
        wq1sb = [ar.alloc([128, HID], BF16) for _ in range(NDC)]
        srepg = ar.alloc([128, 258], F32)
        hraw = ar.alloc([128, 258], F32)
        cbuf = ar.alloc([128, 256], F32)
        h2 = [ar.alloc([128, 256], F32) for _ in range(NHC)]
        racc = ar.alloc([128, 256], F32)
        raccT = ar.alloc([128, 256], F32)
        raccT_off = ar.last
        sxr = ar.alloc([128, 256], F32)
        splf = ar.alloc([128, 128], F32)
        splf_off = ar.last
        splb = ar.alloc([128, 128], BF16)
        amax2 = ar.alloc([128, 2], F32)
        sx2t = ar.alloc([128, 2], F32)
        statT2 = ar.alloc([2, 128], F32)
        sx2row = ar.alloc([1, 256], F32, at=s1row_off)
        zb = ar.alloc([128, 256], F32, at=raccT_off)
        i32g = ar.alloc([128, 256], I32)
        rmin = ar.alloc([128, 256], F32)
        ihg = ar.alloc([128, 256], BF16)
        fxg = ar.alloc([128, 256], BF16, at=splf_off)
        hqs_strip = [ar.alloc([128, 256], BF16) for _ in range(2)]
        psH = [psum(stkA, [128, 512], F32) for _ in range(2)]
        psS = psum(stkA, [128, 512], F32)
        psT3 = psum(stkA, [128, 256], BF16)

        e_w1l = 0
        for k in range(NDC):
            e_w1l = sc.add("sync", lambda h, k=k: h.dma_start(
                out=wq1sb[k][:], in_=wq1d[k * 128:(k + 1) * 128, :]),
                waits=[e_wq1] if k == 0 else (), dma=True)

        e_spill = []
        e_se_prev = 0
        e_sil_prev = 0
        e_deq = {}
        e_i_prev = 0
        e_hq_prev = 0
        e_fx_prev = 0
        e_zb_guard = []
        e_sxr = 0
        e_zm = 0
        qg_scale = {}
        e_sil_hist = {}
        for g in range(NGA + 1):
            conv = g < NGA
            quant = g > 0
            if conv:
                lo = max(g * GA - 1, 0)
                hi_ = min(g * GA + GA + 1, T)
                w = hi_ - lo
                coff = 1 if g == 0 else 0
                wt = [e_statw]
                if e_se_prev:
                    wt.append(e_se_prev)
                e_r1 = sc.add("sync", lambda h, lo=lo, w=w: h.dma_start(
                    out=s1row[0:1, 0:w],
                    in_=statd.rearrange("(a t) -> a t", a=1)[:, lo:lo + w]),
                    waits=wt, dma=True)
                e_bm = sc.add("tensor", lambda h, w=w: h.matmul(
                    out=psS[:, 0:w], lhsT=onesrow[:], rhs=s1row[0:1, 0:w],
                    start=True, stop=True),
                    waits=[e_r1, e_ones] + ([e_se_prev] if e_se_prev else []))
                wt_se = [e_bm]
                if g > 0:
                    wt_se.append(e_deq[(g - 1, NHC - 1)])
                e_se = sc.add("scalar", lambda h, w=w, coff=coff: h.copy(
                    out=srepg[:, coff:coff + w], in_=psS[:, 0:w]), waits=wt_se)
                e_se_prev = e_se
                if g == 0:
                    sc.add("vector", lambda h: h.memset(hraw[:, 0:1], 0.0))
                if g == NGA - 1:
                    sc.add("vector", lambda h: h.memset(hraw[:, 257:258], 0.0))
            for m in range(NHC):
                if conv:
                    ps = psH[m % 2]
                    wt1 = [e_w1l, e_lnq_s]
                    if (g * NHC + m) >= 2:
                        n = g * NHC + m - 2
                        wt1.append(e_deq[(n // NHC, n % NHC)])
                    e_mm = 0
                    for k in range(NDC):
                        e_mm = sc.add("tensor", lambda h, k=k, m=m, lo=lo, w=w, ps=ps: h.matmul(
                            out=ps[:, 0:w],
                            lhsT=wq1sb[k][:, m * 128:(m + 1) * 128],
                            rhs=xqF[k][:, lo:lo + w],
                            start=(k == 0), stop=(k == NDC - 1)),
                            waits=wt1 if k == 0 else [])
                    e_dq = sc.add("vector", lambda h, w=w, coff=coff, ps=ps: h.tensor_tensor(
                        out=hraw[:, coff:coff + w], in0=ps[:, 0:w],
                        in1=srepg[:, coff:coff + w], op=ALU.mult),
                        waits=[e_mm, e_se])
                    e_deq[(g, m)] = e_dq
                    wt_c1 = [e_dq, e_cb]
                    if e_sil_prev:
                        wt_c1.append(e_sil_prev)
                    e_c1 = sc.add("gpsimd", lambda h, m=m: h.tensor_scalar(
                        out=cbuf[:], in0=hraw[:, 0:256], scalar1=cw[:, 3 * m:3 * m + 1],
                        scalar2=cb[:, m:m + 1], op0=ALU.mult, op1=ALU.add),
                        waits=wt_c1)
                if quant:
                    e_sxr_g = qg_scale[g - 1]
                    wt4 = [e_sxr_g] if m == 0 else []
                    if e_i_prev:
                        wt4.append(e_i_prev)
                    if e_fx_prev:
                        wt4.append(e_fx_prev)
                    e_zm = sc.add("gpsimd", lambda h, m=m: h.tensor_tensor(
                        out=zb[:], in0=h2[m][:], in1=sxr[:], op=ALU.mult), waits=wt4)
                if conv:
                    e_c2 = sc.add("vector", lambda h, m=m: h.scalar_tensor_tensor(
                        out=cbuf[:], in0=hraw[:, 1:257], scalar=cw[:, 3 * m + 1:3 * m + 2],
                        op0=ALU.mult, op1=ALU.add, in1=cbuf[:]),
                        waits=[e_c1])
                    e_c3 = sc.add("vector", lambda h, m=m: h.scalar_tensor_tensor(
                        out=cbuf[:], in0=hraw[:, 2:258], scalar=cw[:, 3 * m + 2:3 * m + 3],
                        op0=ALU.mult, op1=ALU.add, in1=cbuf[:]))
                if quant:
                    e_i = sc.add("scalar", lambda h: h.activation(
                        out=i32g[:], in_=zb[:], func=AF.Identity, bias=C("half")),
                        waits=[e_zm] + ([e_hq_prev] if e_hq_prev else []))
                    e_i_prev = e_i
                    e_h = sc.add("scalar", lambda h: h.activation(
                        out=ihg[:], in_=i32g[:], func=AF.Identity, bias=C("neghalf")))
                if conv:
                    wt_sil = [e_c3]
                    if quant:
                        wt_sil.append(e_zm)
                    e_sil = sc.add("scalar", lambda h, m=m: h.activation(
                        out=h2[m][:], in_=cbuf[:], func=AF.Silu),
                        waits=wt_sil)
                    e_sil_prev = e_sil
                    e_sil_hist[m] = e_sil
                if quant:
                    e_fx_prev = sc.add("vector", lambda h: h.tensor_tensor(
                        out=fxg[:], in0=ihg[:], in1=zb[:], op=ALU.is_gt),
                        waits=[e_h])
                    strip = hqs_strip[m % 2]
                    wt_hq = [e_spill[-2]] if len(e_spill) >= 2 else []
                    e_hq = sc.add("vector", lambda h, strip=strip: h.scalar_tensor_tensor(
                        out=strip[:], in0=ihg[:], scalar=0.5, op0=ALU.add,
                        op1=ALU.subtract, in1=fxg[:]), waits=wt_hq)
                    e_hq_prev = e_hq
                    e_spill.append(sc.add("gpsimd", lambda h, g=g, m=m, strip=strip: h.dma_start(
                        out=hqd[m, :, (g - 1) * GA:g * GA], in_=strip[:]),
                        waits=[e_hq], dma=True))
                if conv and m > 0:
                    mm1 = m - 1
                    wt_r = [e_sil_hist[mm1]]
                    if mm1 == 0:
                        e_r = sc.add("vector", lambda h: h.tensor_copy(
                            out=racc[:], in_=h2[0][:]), waits=wt_r)
                        e_rG = sc.add("vector", lambda h: h.tensor_copy(
                            out=rmin[:], in_=h2[0][:]), waits=wt_r)
                    else:
                        e_r = sc.add("vector", lambda h, mm1=mm1: h.tensor_tensor(
                            out=racc[:], in0=racc[:], in1=h2[mm1][:], op=ALU.max),
                            waits=wt_r)
                        e_rG = sc.add("vector", lambda h, mm1=mm1: h.tensor_tensor(
                            out=rmin[:], in0=rmin[:], in1=h2[mm1][:], op=ALU.min),
                            waits=wt_r)
            if quant:
                e_zb_guard = [e_zm, e_i_prev]
            if not conv:
                break
            mm1 = NHC - 1
            e_r = sc.add("vector", lambda h, mm1=mm1: h.tensor_tensor(
                out=racc[:], in0=racc[:], in1=h2[mm1][:], op=ALU.max),
                waits=[e_sil_hist[mm1]])
            e_rG = sc.add("vector", lambda h, mm1=mm1: h.tensor_tensor(
                out=rmin[:], in0=rmin[:], in1=h2[mm1][:], op=ALU.min),
                waits=[e_sil_hist[mm1]])
            e_tT = 0
            for blk in range(2):
                e_tT = fp32_transpose(racc[:, blk * 128:(blk + 1) * 128], 128, 128,
                                      raccT[:, blk * 128:(blk + 1) * 128],
                                      splf[:], splb[:], psT3,
                                      [e_r] + ([e_tT] if e_tT else [])
                                      + (e_zb_guard if blk == 0 else []))
                sc.add("vector", lambda h, blk=blk: h.tensor_reduce(
                    out=amax2[:, blk:blk + 1],
                    in_=raccT[:, blk * 128:(blk + 1) * 128], axis=AX.X,
                    op=ALU.max), waits=[e_tT])
            for blk in range(2):
                e_tT = fp32_transpose(rmin[:, blk * 128:(blk + 1) * 128], 128, 128,
                                      raccT[:, blk * 128:(blk + 1) * 128],
                                      splf[:], splb[:], psT3,
                                      [e_rG, e_tT])
                sc.add("vector", lambda h, blk=blk: h.tensor_reduce(
                    out=sx2t[:, blk:blk + 1],
                    in_=raccT[:, blk * 128:(blk + 1) * 128], axis=AX.X,
                    op=ALU.min), waits=[e_tT])
            sc.add("vector", lambda h: h.tensor_scalar(
                out=sx2t[:], in0=sx2t[:], scalar1=-1.0, scalar2=None,
                op0=ALU.mult))
            sc.add("vector", lambda h: h.tensor_tensor(
                out=amax2[:], in0=amax2[:], in1=sx2t[:], op=ALU.max))
            sc.add("vector", lambda h: h.tensor_scalar(
                out=amax2[:], in0=amax2[:], scalar1=1e-5, scalar2=None,
                op0=ALU.max))
            sc.add("vector", lambda h, g=g: h.tensor_scalar(
                out=invs2c[:, 2 * g:2 * g + 2], in0=amax2[:],
                scalar1=smallv[:, 3:4], scalar2=0.5 / 127.0,
                op0=ALU.mult, op1=ALU.mult))
            sc.add("vector", lambda h: h.reciprocal(out=sx2t[:], in_=amax2[:]))
            e_sx2t = sc.add("vector", lambda h: h.tensor_scalar(
                out=sx2t[:], in0=sx2t[:], scalar1=127.0, scalar2=None,
                op0=ALU.mult))
            e_tT2 = fp32_transpose(sx2t[:], 128, 2, statT2[:],
                                   splf[:, 0:2], splb[:, 0:2], psT3, [e_sx2t])
            e_g2 = 0
            for j in range(2):
                e_g2 = sc.add("tensor", lambda h, j=j: h.matmul(
                    out=psS[0:1, j * 128:(j + 1) * 128],
                    lhsT=ident16[0:2, j:j + 1], rhs=statT2[:],
                    start=True, stop=True),
                    waits=[e_tT2, e_id16, e_se] if j == 0 else [])
            e_rw = sc.add("scalar", lambda h: h.copy(
                out=sx2row[:], in_=psS[0:1, 0:256]), waits=[e_g2])
            e_bm2 = sc.add("tensor", lambda h: h.matmul(
                out=psS[:, 0:256], lhsT=onesrow[:], rhs=sx2row[:],
                start=True, stop=True), waits=[e_rw])
            e_sxr = sc.add("scalar", lambda h: h.copy(
                out=sxr[:], in_=psS[:, 0:256]), waits=[e_bm2])
            qg_scale[g] = e_sxr
            e_se_prev = e_sxr
        psclose()
        ar.reset(mkB)
        B = sc.barrier()

        stkB = psopen()
        wq2sb = [ar.alloc([128, D], BF16) for _ in range(NHC)]
        hqs = [ar.alloc([128, NHC * GB], BF16) for _ in range(2)]
        syc = invs2c
        psY = [psum(stkB, [128, D], F32) for _ in range(2)]

        e_w2l = 0
        for m in range(NHC):
            e_w2l = sc.add("sync", lambda h, m=m: h.dma_start(
                out=wq2sb[m][:], in_=wq2d[m * 128:(m + 1) * 128, :]),
                waits=(B + [e_wq2]) if m == 0 else (), dma=True)

        e_res_hist = []
        e_gdone = []
        e_mm = 0
        for g in range(NGB):
            buf = hqs[g % 2]
            wt = list(B)
            if g >= 2:
                wt.append(e_gdone[g - 2])
            e_hl = 0
            for m in range(NHC):
                e_hl = sc.add("sync", lambda h, g=g, m=m, buf=buf: h.dma_start(
                    out=buf[:, m * GB:(m + 1) * GB],
                    in_=hqd[m, :, g * GB:(g + 1) * GB]),
                    waits=wt if m == 0 else (), dma=True)
            for ts in range(4):
                tch = g * 4 + ts
                ps = psY[ts % 2]
                wt2 = [e_hl, e_w2l]
                if len(e_res_hist) >= 2:
                    wt2.append(e_res_hist[-2])
                for m in range(NHC):
                    for eh in range(2):
                        e_mm = sc.add("tensor", lambda h, m=m, ts=ts, eh=eh, ps=ps, buf=buf: h.matmul(
                            out=ps[:, eh * 512:(eh + 1) * 512],
                            lhsT=buf[:, m * GB + ts * 128:m * GB + (ts + 1) * 128],
                            rhs=wq2sb[m][:, eh * 512:(eh + 1) * 512],
                            start=(m == 0), stop=(m == NHC - 1)),
                            waits=wt2 if (m == 0 and eh == 0) else [])
                e_res_hist.append(sc.add("vector", lambda h, tch=tch, ps=ps: h.scalar_tensor_tensor(
                    out=xt[:, tch * D:(tch + 1) * D], in0=ps[:, :],
                    scalar=syc[:, tch:tch + 1], op0=ALU.mult, op1=ALU.add,
                    in1=xt[:, tch * D:(tch + 1) * D]),
                    waits=[e_mm]))
            e_gdone.append(e_mm)
        psclose()
        ar.reset(mkF)
        sc.barrier()
        return sc.evt("vector")

    def att(e_xrdy):
        mkA = ar.mark()
        xhF = [ar.alloc([128, T], BF16) for _ in range(NDC)]
        e_lnq_s, _ = lnq(False, e_xrdy, xhF)

        stk = psopen()
        obuf = [ar.alloc([128, T], BF16) for _ in range(NDC)]
        mk2 = ar.mark()
        crep = ar.alloc([128, T], F32)
        srep = ar.alloc([128, T], F32)
        wkv = [ar.alloc([128, 3 * 256], BF16) for _ in range(NDC)]
        qkF = ar.alloc([128, T], F32)
        qkf_off = ar.last
        qswp = ar.alloc([128, T], F32)
        qhB = [ar.alloc([128, T], BF16) for _ in range(2)]
        khB = [ar.alloc([128, T], BF16) for _ in range(2)]
        vtok = ar.alloc([128, NKC * 132], BF16)
        expb = [ar.alloc([128, QHW], BF16, at=qkf_off + 2048 * i) for i in range(4)]
        drow = ar.alloc([1, QHW], F32)
        drep = ar.alloc([64, QHW], F32)
        rmst = qswp[0:32, :]
        psS_ = [psum(stk, [128, QHW], F32) for _ in range(2)]
        psO = [psum(stk, [128, QHW], F32) for _ in range(2)]

        e_rs = sc.add("sync", lambda h: h.dma_start(
            out=rmst, in_=roped[0]), waits=[e_ropew1], dma=True)
        for q in range(4):
            sc.add("vector", lambda h, q=q: h.tensor_copy(
                out=srep[32 * q:32 * (q + 1), :], in_=rmst),
                waits=[e_rs] if q == 0 else [])
        e_rc = sc.add("sync", lambda h: h.dma_start(
            out=rmst, in_=roped[1]),
            waits=[e_ropew2, sc.evt("vector")], dma=True)
        for q in range(4):
            sc.add("vector", lambda h, q=q: h.tensor_copy(
                out=crep[32 * q:32 * (q + 1), :], in_=rmst),
                waits=[e_rc] if q == 0 else [])
        sc.add("vector", lambda h: h.tensor_scalar(
            out=srep[0:32, :], in0=srep[0:32, :], scalar1=-1.0, scalar2=None,
            op0=ALU.mult))
        sc.add("vector", lambda h: h.tensor_scalar(
            out=srep[64:96, :], in0=srep[64:96, :], scalar1=-1.0, scalar2=None,
            op0=ALU.mult))
        e_tab = sc.evt("vector")
        e_von = sc.add("vector", lambda h: h.memset(
            vtok.ap().rearrange("p (k c) -> p k c", k=NKC)[:, :, 64:65], 1.0))
        e_von2 = sc.add("vector", lambda h: h.memset(
            vtok.ap().rearrange("p (k c) -> p k c", k=NKC)[:, :, 130:131], 1.0))

        e_on = 0
        for hg in range(NHG):
            Bh = sc.barrier()
            e_wl = 0
            for k in range(NDC):
                for part in range(3):
                    e_wl = sc.add("sync", lambda h, k=k, part=part, hg=hg: h.dma_start(
                        out=wkv[k][:, part * 256:(part + 1) * 256],
                        in_=wqkvd[k * 128:(k + 1) * 128,
                                  part * D + hg * 256:part * D + (hg + 1) * 256]),
                        waits=(Bh + [e_wqkv]) if (k == 0 and part == 0) else (),
                        dma=True)
            e_rope = 0
            e_s = 0
            for part in range(2):
                dstl = qhB if part == 0 else khB
                for mc in range(2):
                    e_ev = 0
                    for tg in range(T // 512):
                        ps = psS_[tg % 2]
                        wt = [e_wl, e_lnq_s] + Bh
                        if tg >= 2:
                            wt.append(e_ev)
                        if tg == 0 and e_rope:
                            wt.append(e_rope)
                            wt.append(e_s)
                        e_mm = 0
                        for k in range(NDC):
                            e_mm = sc.add("tensor", lambda h, k=k, part=part, mc=mc, tg=tg, ps=ps: h.matmul(
                                out=ps[:, 0:512],
                                lhsT=wkv[k][:, part * 256 + mc * 128:part * 256 + (mc + 1) * 128],
                                rhs=xhF[k][:, tg * 512:(tg + 1) * 512],
                                start=(k == 0), stop=(k == NDC - 1)),
                                waits=wt if k == 0 else [])
                        e_ev = sc.add("scalar", lambda h, tg=tg, ps=ps: h.copy(
                            out=qkF[:, tg * 512:(tg + 1) * 512], in_=ps[:, 0:512]),
                            waits=[e_mm])
                    for (a, b2) in ((0, 32), (32, 0), (64, 96), (96, 64)):
                        e_s = sc.add("gpsimd", lambda h, a=a, b2=b2: h.tensor_copy(
                            out=qswp[a:a + 32, :], in_=qkF[b2:b2 + 32, :]),
                            waits=[e_ev, e_tab])
                    sc.add("vector", lambda h: h.tensor_tensor(
                        out=qswp[:], in0=qswp[:], in1=srep[:], op=ALU.mult),
                        waits=[e_s, e_tab])
                    sc.add("vector", lambda h: h.tensor_tensor(
                        out=qkF[:], in0=qkF[:], in1=crep[:], op=ALU.mult),
                        waits=[e_ev])
                    e_rope = sc.add("vector", lambda h, mc=mc, dstl=dstl: h.tensor_tensor(
                        out=dstl[mc][:], in0=qkF[:], in1=qswp[:], op=ALU.add))
            e_qkv_ev = sc.evt("scalar")
            for pr in range(2):
                e_vt = 0
                e_vt_hist = {}
                for tt in range(NTC):
                    ps = psS_[tt % 2]
                    wt = [e_wl, e_qkv_ev,
                          e_rope if pr == 0 else e_on]
                    if pr == 0:
                        wt.append(e_von2)
                    if tt >= 2:
                        wt.append(e_vt_hist[tt - 2])
                    e_mm = 0
                    for k in range(NDC):
                        e_mm = sc.add("tensor", lambda h, k=k, tt=tt, pr=pr, ps=ps: h.matmul(
                            out=ps[:, 0:128],
                            lhsT=xhF[k][:, tt * 128:(tt + 1) * 128],
                            rhs=wkv[k][:, 512 + pr * 128:512 + (pr + 1) * 128],
                            start=(k == 0), stop=(k == NDC - 1)),
                            waits=wt if k == 0 else [])
                    sc.add("scalar", lambda h, tt=tt, ps=ps: h.copy(
                        out=vtok[:, tt * 132:tt * 132 + 64], in_=ps[:, 0:64]),
                        waits=[e_mm])
                    e_vt = sc.add("scalar", lambda h, tt=tt, ps=ps: h.copy(
                        out=vtok[:, tt * 132 + 66:tt * 132 + 130], in_=ps[:, 64:128]),
                        waits=[])
                    e_vt_hist[tt] = e_vt
                for qh in range(NQH):
                    e_exp = {}
                    e_smm_xy = {}
                    e_pv_hist = {}
                    e_pv = 0
                    nqg = QHW // 512
                    for kc in range(NKC):
                        for xy in range(2):
                            wt = []
                            if xy == 0 and kc == 0:
                                wt += [e_vt, e_rope]
                                if qh > 0 or pr > 0 or hg > 0:
                                    wt.append(e_on)
                            if kc > 0:
                                wt.append(e_exp[(kc - 1, xy)])
                            for qg in range(nqg):
                                e_smm_xy[(kc, xy)] = sc.add("tensor", lambda h, pr=pr, qh=qh, kc=kc, xy=xy, qg=qg: h.matmul(
                                    out=psS_[xy][:, qg * 512:(qg + 1) * 512],
                                    lhsT=khB[pr][xy * 64:(xy + 1) * 64,
                                                 kc * 128:(kc + 1) * 128],
                                    rhs=qhB[pr][xy * 64:(xy + 1) * 64,
                                                qh * QHW + qg * 512:qh * QHW + (qg + 1) * 512],
                                    start=True, stop=True,
                                    tile_position=(xy * 64, 0)),
                                    waits=wt if qg == 0 else [])
                        for xy in range(2):
                            eb = expb[(kc % 2) * 2 + xy]
                            wte = [e_smm_xy[(kc, xy)]]
                            if (kc - 2, xy) in e_pv_hist:
                                wte.append(e_pv_hist[(kc - 2, xy)])
                            e_exp[(kc, xy)] = sc.add("scalar", lambda h, xy=xy, eb=eb: h.activation(
                                out=eb[:], in_=psS_[xy][:],
                                func=AF.Exp, scale=0.125, bias=C("neg15")),
                                waits=wte)
                        for xy in range(2):
                            eb = expb[(kc % 2) * 2 + xy]
                            for qg in range(nqg):
                                e_pv = sc.add("tensor", lambda h, xy=xy, qg=qg, kc=kc, eb=eb: h.matmul(
                                    out=psO[xy][0:65, qg * 512:(qg + 1) * 512],
                                    lhsT=vtok[:, kc * 132 + xy * 66:kc * 132 + xy * 66 + 65],
                                    rhs=eb[:, qg * 512:(qg + 1) * 512],
                                    start=(kc == 0), stop=(kc == NKC - 1)),
                                    waits=[e_exp[(kc, xy)]] if qg == 0 else [])
                            e_pv_hist[(kc, xy)] = e_pv
                    for xy in range(2):
                        head = 4 * hg + 2 * pr + xy
                        e_dr = sc.add("vector", lambda h, xy=xy: h.tensor_copy(
                            out=drow[:], in_=psO[xy][64:65, :]),
                            waits=[e_pv])
                        e_rcp = sc.add("vector", lambda h: h.reciprocal(
                            out=drow[:], in_=drow[:]))
                        e_dm = 0
                        for qg in range(QHW // 512):
                            wt5 = [e_rcp, e_ones, e_exp[(NKC - 1, xy)]]
                            if e_on:
                                wt5.append(e_on)
                            e_dm = sc.add("tensor", lambda h, xy=xy, qg=qg: h.matmul(
                                out=psS_[xy][0:64, qg * 512:(qg + 1) * 512],
                                lhsT=onesrow[0:1, 0:64],
                                rhs=drow[0:1, qg * 512:(qg + 1) * 512],
                                start=True, stop=True),
                                waits=wt5 if qg == 0 else [])
                        e_db = sc.add("scalar", lambda h, xy=xy: h.copy(
                            out=drep[:], in_=psS_[xy][0:64, :]), waits=[e_dm])
                        e_on = sc.add("vector", lambda h, xy=xy, head=head, qh=qh: h.tensor_tensor(
                            out=obuf[head // 2][(head % 2) * 64:(head % 2) * 64 + 64,
                                                qh * QHW:(qh + 1) * QHW],
                            in0=psO[xy][0:64, :], in1=drep[:], op=ALU.mult),
                            waits=[e_db])
        Bo = sc.barrier()
        ar.reset(mk2)
        psX = psS_ if QHW == D else [psum(stk, [128, D], F32) for _ in range(2)]
        wob = [ar.alloc([128, D], BF16) for _ in range(NDC)]
        e_wol = 0
        for k in range(NDC):
            e_wol = sc.add("sync", lambda h, k=k: h.dma_start(
                out=wob[k][:], in_=wod[k * 128:(k + 1) * 128, :]),
                waits=(Bo + [e_wo]) if k == 0 else (), dma=True)
        e_resh = []
        for i in range(NTC):
            ps = psX[i % 2]
            wt = [e_wol, e_on]
            if len(e_resh) >= 2:
                wt.append(e_resh[-2])
            e_mm = 0
            for k in range(NDC):
                for eh in range(2):
                    e_mm = sc.add("tensor", lambda h, i=i, k=k, eh=eh, ps=ps: h.matmul(
                        out=ps[:, eh * 512:(eh + 1) * 512],
                        lhsT=obuf[k][:, i * 128:(i + 1) * 128],
                        rhs=wob[k][:, eh * 512:(eh + 1) * 512],
                        start=(k == 0), stop=(k == NDC - 1)),
                        waits=wt if (k == 0 and eh == 0) else [])
            e_resh.append(sc.add("vector", lambda h, i=i, ps=ps: h.tensor_tensor(
                out=xt[:, i * D:(i + 1) * D], in0=ps[:, :],
                in1=xt[:, i * D:(i + 1) * D], op=ALU.add),
                waits=[e_mm]))
        psclose()
        ar.reset(mkA)
        sc.barrier()
        return sc.evt("vector")

    e_store = 0
    e_xrdy = e_xload
    for b in range(2):
        ffn(e_xrdy)
        att(e_xrdy)
        e_done = ffn(e_xrdy)
        for i in range(NTC):
            e_store = sc.add("gpsimd", lambda h, b=b, i=i: h.dma_start(
                out=x_out[b, i * 128:(i + 1) * 128, :],
                in_=xt[:, i * D:(i + 1) * D]),
                waits=[e_done] if i == 0 else (), dma=True)
        if b == 0:
            e_xrdy = load_x(1, waits=[e_store])

    sc.add("sync", "SEMINC", waits=[e_store])

    with ExitStack() as stk:
        sems = {k: stk.enter_context(nc.semaphore(f"sem_{k}")) for k in Sched.KEYS}
        block = stk.enter_context(nc.Block())
        sc.emit(block, sems)
    return nc


_NC_CACHE = [None]


def kernel(L0_attn_x, L0_attn_w_qkv, L0_attn_w_o, L0_ffn_w1, L0_ffn_conv_w,
           L0_ffn_conv_b, L0_ffn_w2):
    x = np.ascontiguousarray(L0_attn_x, dtype=np.float32)
    wqkv = np.ascontiguousarray(L0_attn_w_qkv, dtype=np.float32)
    wo = np.ascontiguousarray(L0_attn_w_o, dtype=np.float32)
    w1t = np.ascontiguousarray(np.asarray(L0_ffn_w1).T, dtype=np.float32)
    w2t = np.ascontiguousarray(np.asarray(L0_ffn_w2).T, dtype=np.float32)
    cwv = np.ascontiguousarray(
        np.asarray(L0_ffn_conv_w).reshape(NHC, 128, 3).transpose(1, 0, 2)
        .reshape(128, NHC * 3), dtype=np.float32)
    cbv = np.ascontiguousarray(
        np.asarray(L0_ffn_conv_b).reshape(NHC, 128).T, dtype=np.float32)
    ident = np.eye(128, dtype=ml_dtypes.bfloat16)
    ident16 = np.eye(16, dtype=np.float32)
    consts = np.zeros((128, 8), np.float32)
    consts[:, CN["half"]] = 0.5
    consts[:, CN["neghalf"]] = -0.5
    consts[:, CN["neg15"]] = -15.0
    consts[:, CN["one"]] = 1.0

    if _NC_CACHE[0] is None:
        _NC_CACHE[0] = build_program()
    nc = _NC_CACHE[0]

    in_maps = []
    for c in range(8):
        in_maps.append({
            "x_in": x[2 * c:2 * c + 2],
            "w_qkv": wqkv, "w_o": wo, "w1t": w1t, "w2t": w2t,
            "cwt": cwv, "cbt": cbv, "identin": ident, "ident16in": ident16,
            "constin": consts,
        })
    res = run_bass_kernel_spmd(nc, in_maps, core_ids=list(range(8)))
    out = np.concatenate([r["x_out"] for r in res.results], axis=0)
    return np.ascontiguousarray(out, dtype=np.float32)
